# version 01
# baseline (speedup 1.0000x reference)
import numpy as np
import ml_dtypes
import concourse.bass as bass
import concourse.mybir as mybir
from concourse.bass_utils import run_bass_kernel_spmd

F32 = mybir.dt.float32
BF16 = mybir.dt.bfloat16
AF = mybir.ActivationFunctionType
ALU = mybir.AluOpType

D = 4096
S = 8192
HKV = 8
HQ = 32
HD = 128
HID = 11008
RK = 16
CH = 512
NCH = S // CH
DC = D // 128
HC = HID // 128
EPS = 1e-5
EXP_SHIFT = -12.0
NCORES = 8
STOP_AFTER = 7
P1_TILES = None
P1_NO_K = False
P1_NO_XLN = False
P1_NO_ROPE = False
P1_NO_KST = False


class SemC:

    def __init__(self, h):
        self.h = h
        self.n = 0

    def inc(self, k=1):
        self.n += k
        return self.n


NDMA_G = 5
NDMA_S = 9


def rank_chunks(r):
    return r, 15 - r


def build_program(debug=False):
    nc = bass.Bass("TRN2", target_bir_lowering=False,
                   detect_race_conditions=False)

    t_in = {}

    def din(name, shape, dt=F32):
        t_in[name] = nc.dram_tensor(name, shape, dt, kind="ExternalInput")
        return t_in[name]

    xT = din("xT", [D, S])
    wq = din("wq", [D, D])
    wk = din("wk", [D, HKV * HD])
    wv = din("wv", [D, HKV * HD])
    wo = din("wo", [D, D])
    gwT = din("gwT", [D, HID])
    uwT = din("uwT", [D, HID])
    dwT = din("dwT", [HID, D])
    gaT = din("gaT", [D, RK])
    uaT = din("uaT", [D, RK])
    daT = din("daT", [HID, RK])
    gubT = din("gubT", [RK, 2 * HID])
    dbT = din("dbT", [RK, D])
    cosq = din("cosq", [HD, 1024])
    sinq = din("sinq", [HD, 1024])
    cosk = din("cosk", [HD, S])
    sink = din("sink", [HD, S])
    tri = din("tri", [128, 128], BF16)
    rsw = din("rsw", [128, 128], BF16)

    out_fm = nc.dram_tensor("out_fm", [D, 1024], F32, kind="ExternalOutput")

    dbg_kind = "ExternalOutput" if debug else "Internal"
    xln_d = nc.dram_tensor("xln_d", [D, S], BF16, kind=dbg_kind)
    kvk_d = nc.dram_tensor("kvk_d", [HKV * HD, S], BF16, kind=dbg_kind)
    kvv_d = nc.dram_tensor("kvv_d", [S, HKV * HD], BF16, kind=dbg_kind)
    q_d = nc.dram_tensor("q_d", [D, 1024], BF16, kind=dbg_kind)
    o_d = nc.dram_tensor("o_d", [D, 1024], BF16, kind=dbg_kind)
    l_d = nc.dram_tensor("l_d", [HQ, 1024], F32, kind=dbg_kind)
    xattn_d = nc.dram_tensor("xattn_d", [D, 1024], F32, kind=dbg_kind)
    xln2_d = nc.dram_tensor("xln2_d", [D, 1024], BF16, kind=dbg_kind)

    tens = dict(xT=xT, wq=wq, wk=wk, wv=wv, wo=wo, gwT=gwT, uwT=uwT, dwT=dwT,
                gaT=gaT, uaT=uaT, daT=daT, gubT=gubT, dbT=dbT,
                cosq=cosq, sinq=sinq, cosk=cosk, sink=sink, tri=tri, rsw=rsw,
                out_fm=out_fm, xln_d=xln_d, kvk_d=kvk_d, kvv_d=kvv_d,
                q_d=q_d, o_d=o_d, l_d=l_d, xattn_d=xattn_d, xln2_d=xln2_d)

    engines = [nc.tensor, nc.vector, nc.scalar, nc.gpsimd, nc.sync]
    pid = nc.partition_id(engines=[e.engine for e in engines])

    sems = dict(
        pe=nc.alloc_semaphore("s_pe"),
        ve=nc.alloc_semaphore("s_ve"),
        se=nc.alloc_semaphore("s_se"),
        bar=[nc.alloc_semaphore(f"s_bar{i}") for i in range(5)],
        dg=[nc.alloc_semaphore(f"s_dg{i}") for i in range(NDMA_G)],
        dsy=[nc.alloc_semaphore(f"s_dsy{i}") for i in range(NDMA_S)],
    )

    for r in nc.Switch(engines=engines, index=[pid] * 5, n=8):
        emit_rank(nc, r, tens, sems)

    return nc


def emit_rank(nc, r, T, sems):
    lo, hi = rank_chunks(r)
    NT = 16 - r
    P2 = NT * CH
    lo0, hi0 = lo * CH, hi * CH

    TE, VE, SE, GP, SY = nc.tensor, nc.vector, nc.scalar, nc.gpsimd, nc.sync

    pe = SemC(sems["pe"])
    ve = SemC(sems["ve"])
    se = SemC(sems["se"])
    dg = [SemC(h) for h in sems["dg"]]
    dsy = [SemC(h) for h in sems["dsy"]]
    bar = [SemC(h) for h in sems["bar"]]
    _bk = [0]

    def barrier():
        _bk[0] += 1
        k = _bk[0]
        TE.nop().then_inc(bar[0].h, 1)
        VE.memset(dummy_v[0:1, 0:1], 0.0).then_inc(bar[1].h, 1)
        SE.memzero(dummy_s[0:1, 0:1]).then_inc(bar[2].h, 1)
        for d in dg:
            GP.wait_ge(d.h, d.n)
        for d in dsy:
            SY.wait_ge(d.h, d.n)
        GP.nop().then_inc(bar[3].h, 1)
        SY.nop().then_inc(bar[4].h, 1)
        for i, e in enumerate([TE, VE, SE, GP, SY]):
            for j in range(5):
                if j != i:
                    e.wait_ge(bar[j].h, k)

    xTap, wqap, wkap, wvap, woap = (T[k].ap() for k in
                                    ("xT", "wq", "wk", "wv", "wo"))

    with (
        nc.sbuf_tensor(f"dummy{r}", [1, 8], F32) as dummy_v,
        nc.sbuf_tensor(f"dummys{r}", [1, 8], F32) as dummy_s,
        nc.sbuf_tensor(f"ones1b{r}", [128, 1], BF16) as ones1b,
        nc.sbuf_tensor(f"ones1f{r}", [1, 128], F32) as ones1f,
        nc.sbuf_tensor(f"onescf{r}", [128, 1], F32) as onescf,
        nc.sbuf_tensor(f"ebias{r}", [128, 1], F32) as ebias,
        nc.sbuf_tensor(f"epst{r}", [1, 1], F32) as epst,
        nc.sbuf_tensor(f"tri{r}", [128, 128], BF16) as tri_sb,
        nc.sbuf_tensor(f"rsw{r}", [128, 128], BF16) as rsw_sb,
    ):
        VE.memset(ones1b[:, :], 1.0)
        VE.memset(ones1f[:, :], 1.0)
        VE.memset(onescf[:, :], 1.0)
        VE.memset(ebias[:, :], EXP_SHIFT)
        VE.memset(epst[:, :], EPS).then_inc(ve.h, 1)
        ve.inc()
        SY.dma_start(out=tri_sb[:, :], in_=T["tri"].ap()).then_inc(dsy[8].h, 16)
        SY.dma_start(out=rsw_sb[:, :], in_=T["rsw"].ap()).then_inc(dsy[8].h, 16)
        dsy[8].inc(32)
        TE.wait_ge(dsy[8].h, dsy[8].n)
        TE.wait_ge(ve.h, ve.n)
        SE.wait_ge(ve.h, ve.n)
        VE.wait_ge(dsy[8].h, dsy[8].n)

        if STOP_AFTER >= 1:
            phase1(nc, r, T, locals())
            barrier()
        if STOP_AFTER >= 2:
            phase2(nc, r, T, locals())
            barrier()
        if STOP_AFTER >= 3:
            phase3(nc, r, T, locals())
            barrier()
        if STOP_AFTER >= 4:
            phase4(nc, r, T, locals())
            barrier()
        if STOP_AFTER >= 5:
            phase56(nc, r, T, locals())
            barrier()
        if STOP_AFTER >= 7:
            phase7(nc, r, T, locals())

        for d in dg:
            GP.wait_ge(d.h, d.n)
        for d in dsy:
            SY.wait_ge(d.h, d.n)


def phase1(nc, r, T, env):
    TE, VE, SE, GP, SY = nc.tensor, nc.vector, nc.scalar, nc.gpsimd, nc.sync
    dg, dsy = env["dg"], env["dsy"]
    pe, ve, se = env["pe"], env["ve"], env["se"]
    ones1b, ones1f, epst = env["ones1b"], env["ones1f"], env["epst"]
    NT = env["NT"]
    s_wk, s_xt = dg[0], dg[1]
    s_rope = [dsy[0], dsy[1]]
    s_xlnst = [dsy[2], dsy[3]]
    s_kst = [dsy[4], dsy[5]]

    with (
        nc.sbuf_tensor(f"p1wk{r}", [128, DC * HKV * HD], BF16) as wk_sb,
        nc.sbuf_tensor(f"p1xt{r}", [128, DC * CH], BF16) as xt,
        nc.sbuf_tensor(f"p1xsq{r}", [128, 2 * CH], BF16) as xsq,
        nc.sbuf_tensor(f"p1xln{r}", [128, 2 * DC * CH], BF16) as xlnb,
        nc.sbuf_tensor(f"p1tmp{r}", [128, CH], BF16) as tmp,
        nc.sbuf_tensor(f"p1tmpf{r}", [128, 2 * CH], F32) as tmpf,
        nc.sbuf_tensor(f"p1kc{r}", [128, CH], BF16) as kc_sb,
        nc.sbuf_tensor(f"p1rows{r}", [1, 6 * CH], F32) as rows,
        nc.sbuf_tensor(f"p1rope{r}", [128, 2 * CH], F32) as ropeb,
        nc.sbuf_tensor(f"p1kst{r}", [128, 2 * HKV * CH], BF16) as kst,
        nc.psum_tensor(f"p1sum{r}", [1, CH], F32) as sum_ps,
        nc.psum_tensor(f"p1sq{r}", [1, CH], F32) as sq_ps,
        nc.psum_tensor(f"p1ab{r}", [128, CH], F32) as ab_ps,
        nc.psum_tensor(f"p1bb{r}", [128, CH], F32) as bb_ps,
        nc.psum_tensor(f"p1k0{r}", [128, CH], F32) as k0_ps,
        nc.psum_tensor(f"p1k1{r}", [128, CH], F32) as k1_ps,
        nc.psum_tensor(f"p1kr{r}", [128, CH], F32) as kr_ps,
    ):
        kps = [k0_ps, k1_ps]
        rsw_sb = env["rsw_sb"]
        GP.dma_start(out=wk_sb[:, :].rearrange("p (c o) -> p c o", c=DC),
                     in_=T["wk"].ap().rearrange("(c p) o -> p c o", p=128)
                     ).then_inc(s_wk.h, 16)
        s_wk.inc(16)
        TE.wait_ge(s_wk.h, s_wk.n)

        xln_done = [None] * NT
        kdma_done = [None] * NT
        rope_done = [None] * NT
        stats_read = [None] * NT
        xlnst_ev = [None] * NT

        sq_cnt = 0

        NTl = min(NT, P1_TILES) if P1_TILES else NT
        for t in range(NTl):
            tc = bass.ts(t, CH)
            if t > 0:
                GP.wait_ge(ve.h, xln_done[t - 1])
            GP.dma_start(out=xt[:, :].rearrange("p (c w) -> p c w", c=DC),
                         in_=xTap_slice(T, tc)).then_inc(s_xt.h, 16)
            s_xt.inc(16)
            ld = s_xt.n

            SE.wait_ge(s_xt.h, ld)
            TE.wait_ge(s_xt.h, ld)
            if t > 0:
                TE.wait_ge(ve.h, stats_read[t - 1])
            for c in range(DC):
                xc = xt[:, bass.ts(c, CH)]
                xq = xsq[:, bass.ts(c % 2, CH)]
                if c >= 2:
                    SE.wait_ge(pe.h, mmsq_base + (c - 2) + 1)
                SE.square(xq, xc).then_inc(se.h, 1)
                sq_cnt = se.inc()
                TE.matmul(sum_ps[:, :], ones1b[:, :], xc,
                          start=(c == 0), stop=(c == DC - 1))
                TE.wait_ge(se.h, sq_cnt)
                if c == 0:
                    mmsq_base = pe.n
                TE.matmul(sq_ps[:, :], ones1b[:, :], xq,
                          start=(c == 0), stop=(c == DC - 1)).then_inc(pe.h, 1)
                pe.inc()
            stats_done = pe.n

            mean = rows[:, 0:CH]
            ex2 = rows[:, CH:2 * CH]
            var = rows[:, 2 * CH:3 * CH]
            sqv = rows[:, 3 * CH:4 * CH]
            a_row = rows[:, 4 * CH:5 * CH]
            b_row = rows[:, 5 * CH:6 * CH]
            VE.wait_ge(pe.h, stats_done)
            VE.tensor_scalar_mul(mean, sum_ps[:, :], 1.0 / D)
            VE.tensor_scalar_mul(ex2, sq_ps[:, :], 1.0 / D)
            VE.engine_nop().then_inc(ve.h, 1)
            stats_read[t] = ve.inc()
            VE.tensor_mul(var, mean, mean)
            VE.tensor_sub(var, ex2, var).then_inc(ve.h, 1)
            vv = ve.inc()
            SE.wait_ge(ve.h, vv)
            SE.activation(sqv, var, AF.Sqrt, bias=epst[:, :]).then_inc(se.h, 1)
            sv = se.inc()
            VE.wait_ge(se.h, sv)
            VE.reciprocal(a_row, sqv)
            VE.tensor_mul(b_row, mean, a_row).then_inc(ve.h, 1)
            rows_done = ve.inc()

            TE.wait_ge(ve.h, rows_done)
            if t > 0:
                TE.wait_ge(ve.h, xln_done[t - 1])
            TE.matmul(ab_ps[:, :], ones1f[:, :], a_row, start=True, stop=True)
            TE.matmul(bb_ps[:, :], ones1f[:, :], b_row,
                      start=True, stop=True).then_inc(pe.h, 1)
            bcast_done = pe.inc()

            if P1_NO_XLN:
                VE.engine_nop().then_inc(ve.h, 1)
                xln_done[t] = ve.inc()
                VE.engine_nop().then_inc(ve.h, 1)
                rope_done[t] = ve.inc()
                kproj_done[t] = pe.n if t > 0 else pe.n
                if t == 0:
                    kproj_done = [pe.n] * NT
                continue
            xl = xlnb[:, bass.ts(t % 2, DC * CH)]
            VE.wait_ge(pe.h, bcast_done)
            if t > 1:
                VE.wait_ge(pe.h, kproj_done[t - 2])
                VE.wait_ge(s_xlnst[t % 2].h, xlnst_ev[t - 2])
            for c in range(DC):
                xc = xt[:, bass.ts(c, CH)]
                VE.tensor_mul(tmp[:, :], xc, ab_ps[:, :])
                VE.tensor_sub(xl[:, bass.ts(c, CH)], tmp[:, :], bb_ps[:, :])
            VE.engine_nop().then_inc(ve.h, 1)
            xln_done[t] = ve.inc()

            SY.wait_ge(ve.h, xln_done[t])
            SY.dma_start(out=T["xln_d"].ap()[:, tc].rearrange(
                             "(c p) w -> p c w", p=128),
                         in_=xl.rearrange("p (c w) -> p c w", c=DC)
                         ).then_inc(s_xlnst[t % 2].h, 16)
            xlnst_ev[t] = s_xlnst[t % 2].inc(16)

            rope_cos = ropeb[:, 0:CH]
            rope_sin = ropeb[:, CH:2 * CH]
            if t > 0:
                SY.wait_ge(ve.h, rope_done[t - 1])
            sr = s_rope[0]
            SY.dma_start(out=rope_cos, in_=T["cosk"].ap()[:, tc]).then_inc(sr.h, 16)
            SY.dma_start(out=rope_sin, in_=T["sink"].ap()[:, tc]).then_inc(sr.h, 16)
            sr.inc(32)
            rope_ld = sr.n

            if t == 0:
                kproj_done = [None] * NT
            if P1_NO_K:
                VE.engine_nop().then_inc(ve.h, 1)
                rope_done[t] = ve.inc()
                kproj_done[t] = pe.n
                continue
            ks = kst[:, bass.ts(t % 2, HKV * CH)]
            if t > 1:
                VE.wait_ge(s_kst[t % 2].h, kdma_done[t - 2])
            if t > 0:
                TE.wait_ge(ve.h, rope_done[t - 1])
            TE.wait_ge(ve.h, xln_done[t])
            kc_ev = {}
            rot_mm = {}
            rope_ev = {}
            for h in range(HKV):
                kp = kps[h % 2]
                if h >= 2:
                    TE.wait_ge(ve.h, kc_ev[h - 2])
                for c in range(DC):
                    ins = TE.matmul(kp[:, :],
                                    wk_sb[:, c * HKV * HD + h * HD:
                                          c * HKV * HD + (h + 1) * HD],
                                    xl[:, bass.ts(c, CH)],
                                    start=(c == 0), stop=(c == DC - 1))
                ins.then_inc(pe.h, 1)
                kmm = pe.inc()
                if P1_NO_ROPE:
                    ko = ks[:, bass.ts(h, CH)]
                    VE.wait_ge(pe.h, kmm)
                    VE.tensor_copy(ko, kp[:, :]).then_inc(ve.h, 1)
                    kc_ev[h] = ve.inc()
                    rope_ev[h] = kc_ev[h]
                    rot_mm[h] = pe.n
                    continue
                VE.wait_ge(pe.h, kmm)
                if h >= 1:
                    VE.wait_ge(pe.h, rot_mm[h - 1])
                elif t > 0:
                    VE.wait_ge(pe.h, last_rot_mm)
                VE.tensor_copy(kc_sb[:, :], kp[:, :]).then_inc(ve.h, 1)
                kc_ev[h] = ve.inc()
                TE.wait_ge(ve.h, kc_ev[h])
                if h >= 1:
                    TE.wait_ge(ve.h, rope_ev[h - 1])
                TE.matmul(kr_ps[:, :], rsw_sb[:, :], kc_sb[:, :],
                          start=True, stop=True).then_inc(pe.h, 1)
                rot_mm[h] = pe.inc()
                ko = ks[:, bass.ts(h, CH)]
                VE.wait_ge(pe.h, rot_mm[h])
                VE.wait_ge(sr.h, rope_ld)
                VE.tensor_mul(tmpf[:, 0:CH], kc_sb[:, :], rope_cos)
                VE.tensor_mul(tmpf[:, CH:2 * CH], kr_ps[:, :], rope_sin)
                VE.tensor_add(ko, tmpf[:, 0:CH], tmpf[:, CH:2 * CH]
                              ).then_inc(ve.h, 1)
                rope_ev[h] = ve.inc()
            last_rot_mm = rot_mm[HKV - 1]
            rope_done[t] = ve.n
            kproj_done[t] = pe.n

            if P1_NO_KST:
                kdma_done[t] = s_kst[t % 2].n
                continue
            SY.wait_ge(ve.h, rope_done[t])
            SY.dma_start(out=T["kvk_d"].ap()[:, tc].rearrange(
                             "(h p) w -> p h w", p=128),
                         in_=ks.rearrange("p (h w) -> p h w", h=HKV)
                         ).then_inc(s_kst[t % 2].h, 16)
            kdma_done[t] = s_kst[t % 2].inc(16)


def xTap_slice(T, tc):
    return T["xT"].ap()[:, tc].rearrange("(c p) w -> p c w", p=128)


def phase2(nc, r, T, env):
    TE, VE, SE, GP, SY = nc.tensor, nc.vector, nc.scalar, nc.gpsimd, nc.sync
    dg, dsy = env["dg"], env["dsy"]
    pe, ve = env["pe"], env["ve"]
    NT = env["NT"]
    KV = HKV * HD
    s_wv = dg[0]
    s_xln = [dsy[0], dsy[1]]
    s_vst = [dsy[2], dsy[3]]

    with (
        nc.sbuf_tensor(f"p2wv{r}", [128, DC * KV], BF16) as wv_sb,
        nc.sbuf_tensor(f"p2xln{r}", [128, 2 * DC * CH], BF16) as xlnb,
        nc.sbuf_tensor(f"p2vst{r}", [128, 2 * KV], BF16) as vst,
        nc.psum_tensor(f"p2v0{r}", [128, KV], F32) as v0,
        nc.psum_tensor(f"p2v1{r}", [128, KV], F32) as v1,
    ):
        vps = [v0, v1]
        GP.dma_start(out=wv_sb[:, :].rearrange("p (c o) -> p c o", c=DC),
                     in_=T["wv"].ap().rearrange("(c p) o -> p c o", p=128)
                     ).then_inc(s_wv.h, 16)
        TE.wait_ge(s_wv.h, s_wv.inc(16))

        xln_ld = [None] * NT
        ev_done = {}
        vst_dma = {}
        for t in range(NT):
            tc = bass.ts(t, CH)
            xl = xlnb[:, bass.ts(t % 2, DC * CH)]
            if t > 1:
                SY.wait_ge(pe.h, mm_done[t - 2])
            SY.dma_start(out=xl.rearrange("p (c w) -> p c w", c=DC),
                         in_=T["xln_d"].ap()[:, tc].rearrange(
                             "(c p) w -> p c w", p=128)
                         ).then_inc(s_xln[t % 2].h, 16)
            xln_ld[t] = s_xln[t % 2].inc(16)
            if t == 0:
                mm_done = [None] * NT
            TE.wait_ge(s_xln[t % 2].h, xln_ld[t])
            for s in range(4):
                vp = vps[s % 2]
                key = (t, s)
                prev = (t, s - 2) if s >= 2 else (t - 1, s + 2)
                if prev in ev_done:
                    TE.wait_ge(ve.h, ev_done[prev])
                for c in range(DC):
                    lhsT = xl[:, c * CH + s * 128: c * CH + (s + 1) * 128]
                    for half in range(2):
                        ins = TE.matmul(vp[:, bass.ts(half, CH)],
                                        lhsT,
                                        wv_sb[:, c * KV + half * CH:
                                              c * KV + (half + 1) * CH],
                                        start=(c == 0), stop=(c == DC - 1))
                ins.then_inc(pe.h, 1)
                mmv = pe.inc()
                vs = vst[:, bass.ts(s % 2, KV)]
                pk = (t, s - 2) if s >= 2 else (t - 1, s + 2)
                if pk in vst_dma:
                    VE.wait_ge(s_vst[s % 2].h, vst_dma[pk])
                VE.wait_ge(pe.h, mmv)
                VE.tensor_copy(vs, vp[:, :]).then_inc(ve.h, 1)
                ev_done[key] = ve.inc()
                SY.wait_ge(ve.h, ev_done[key])
                SY.dma_start(out=T["kvv_d"].ap()[t * CH + s * 128:
                                                 t * CH + (s + 1) * 128, :],
                             in_=vs).then_inc(s_vst[s % 2].h, 16)
                vst_dma[key] = s_vst[s % 2].inc(16)
            mm_done[t] = pe.n


def phase3(nc, r, T, env):
    TE, VE, SE, GP, SY = nc.tensor, nc.vector, nc.scalar, nc.gpsimd, nc.sync
    dg, dsy = env["dg"], env["dsy"]
    pe, ve = env["pe"], env["ve"]
    NT, lo = env["NT"], env["lo"]
    s_xo = dsy[0]
    s_wq = [dg[0], dg[1]]
    s_qst = [dsy[1], dsy[2]]

    with (
        nc.sbuf_tensor(f"p3xln{r}", [128, DC * 1024], BF16) as xo,
        nc.sbuf_tensor(f"p3wq{r}", [128, 2 * DC * HD], BF16) as wqb,
        nc.sbuf_tensor(f"p3rope{r}", [128, 2 * 1024], F32) as ropeq,
        nc.sbuf_tensor(f"p3tmp{r}", [128, 2 * 1024], F32) as tmp,
        nc.sbuf_tensor(f"p3qc{r}", [128, 1024], BF16) as qc_sb,
        nc.sbuf_tensor(f"p3qst{r}", [128, 2 * 1024], BF16) as qst,
        nc.psum_tensor(f"p3q0{r}", [128, 1024], F32) as q0,
        nc.psum_tensor(f"p3q1{r}", [128, 1024], F32) as q1,
        nc.psum_tensor(f"p3qr{r}", [128, 1024], F32) as qr_ps,
    ):
        qps = [q0, q1]
        rsw_sb = env["rsw_sb"]
        for i, t in enumerate((lo, NT - 1)):
            SY.dma_start(
                out=xo[:, :].rearrange("p (c u w) -> p c u w", c=DC, u=2
                                       )[:, :, i, :],
                in_=T["xln_d"].ap()[:, bass.ts(t, CH)].rearrange(
                    "(c p) w -> p c w", p=128)).then_inc(s_xo.h, 16)
            s_xo.inc(16)
        SY.dma_start(out=ropeq[:, 0:1024], in_=T["cosq"].ap()).then_inc(s_xo.h, 16)
        SY.dma_start(out=ropeq[:, 1024:2048], in_=T["sinq"].ap()).then_inc(s_xo.h, 16)
        xo_ld = s_xo.inc(32)
        cosb, sinb = ropeq[:, 0:1024], ropeq[:, 1024:2048]

        TE.wait_ge(s_xo.h, xo_ld)
        VE.wait_ge(s_xo.h, xo_ld)
        rope_ev = {}
        qdma_ev = {}
        qc_ev = {}
        rot_mm = {}
        for oc in range(HQ):
            wqt = wqb[:, bass.ts(oc % 2, DC * HD)]
            if oc >= 2:
                GP.wait_ge(pe.h, qmm_ev[oc - 2])
            GP.dma_start(out=wqt.rearrange("p (c o) -> p c o", c=DC),
                         in_=T["wq"].ap()[:, bass.ts(oc, HD)].rearrange(
                             "(c p) o -> p c o", p=128)
                         ).then_inc(s_wq[oc % 2].h, 16)
            wq_ld = s_wq[oc % 2].inc(16)
            qp = qps[oc % 2]
            if oc == 0:
                qmm_ev = [None] * HQ
            TE.wait_ge(s_wq[oc % 2].h, wq_ld)
            if oc >= 2:
                TE.wait_ge(ve.h, qc_ev[oc - 2])
            for c in range(DC):
                for half in range(2):
                    ins = TE.matmul(qp[:, bass.ts(half, CH)],
                                    wqt[:, bass.ts(c, HD)],
                                    xo[:, c * 1024 + half * CH:
                                       c * 1024 + (half + 1) * CH],
                                    start=(c == 0), stop=(c == DC - 1))
            ins.then_inc(pe.h, 1)
            qmm_ev[oc] = pe.inc()
            VE.wait_ge(pe.h, qmm_ev[oc])
            if oc >= 1:
                VE.wait_ge(pe.h, rot_mm[oc - 1])
            VE.tensor_copy(qc_sb[:, :], qp[:, :]).then_inc(ve.h, 1)
            qc_ev[oc] = ve.inc()
            TE.wait_ge(ve.h, qc_ev[oc])
            if oc >= 1:
                TE.wait_ge(ve.h, rope_ev[oc - 1])
            for half in range(2):
                ins = TE.matmul(qr_ps[:, bass.ts(half, CH)], rsw_sb[:, :],
                                qc_sb[:, bass.ts(half, CH)],
                                start=True, stop=True)
            ins.then_inc(pe.h, 1)
            rot_mm[oc] = pe.inc()
            qo = qst[:, bass.ts(oc % 2, 1024)]
            VE.wait_ge(pe.h, rot_mm[oc])
            if oc >= 2:
                VE.wait_ge(s_qst[oc % 2].h, qdma_ev[oc - 2])
            VE.tensor_mul(tmp[:, 0:1024], qc_sb[:, :], cosb)
            VE.tensor_mul(tmp[:, 1024:2048], qr_ps[:, :], sinb)
            VE.tensor_add(qo, tmp[:, 0:1024], tmp[:, 1024:2048]
                          ).then_inc(ve.h, 1)
            rope_ev[oc] = ve.inc()
            SY.wait_ge(ve.h, rope_ev[oc])
            SY.dma_start(out=T["q_d"].ap()[bass.ts(oc, HD), :],
                         in_=qo).then_inc(s_qst[oc % 2].h, 16)
            qdma_ev[oc] = s_qst[oc % 2].inc(16)


def spans_for(c, r):
    pieces = []
    tris = []
    if c < 4 * r:
        pieces.append((0, 512))
    elif c < 4 * r + 4:
        j0 = 128 * (c - 4 * r)
        pieces.append((j0, 512))
        tris.append(j0)
    ch = c - 4 * (15 - r)
    if ch < 0:
        pieces.append((512, 1024))
    elif ch < 4:
        j0 = 128 * ch
        pieces.append((512 + j0, 1024))
        tris.append(512 + j0)
    merged = []
    for p in pieces:
        if merged and merged[-1][1] == p[0]:
            merged[-1] = (merged[-1][0], p[1])
        else:
            merged.append(list(p) and p)
    return merged, tris


def phase4(nc, r, T, env):
    TE, VE, SE, GP, SY = nc.tensor, nc.vector, nc.scalar, nc.gpsimd, nc.sync
    dg, dsy = env["dg"], env["dsy"]
    pe, ve, se = env["pe"], env["ve"], env["se"]
    NT, P2 = env["NT"], env["P2"]
    ones1b, ones1f, ebias, tri_sb = (env["ones1b"], env["ones1f"],
                                     env["ebias"], env["tri_sb"])
    NKC = P2 // 128
    s_kv = [dsy[0], dsy[1]]
    s_q = [dsy[2], dsy[3]]
    s_ost = [dsy[4], dsy[5]]
    s_ldbg = dsy[6]

    with (
        nc.sbuf_tensor(f"p4k{r}", [128, 2 * S], BF16) as kbuf,
        nc.sbuf_tensor(f"p4v{r}", [128, 2 * S], BF16) as vbuf,
        nc.sbuf_tensor(f"p4q{r}", [128, 2 * 1024], BF16) as qbuf,
        nc.sbuf_tensor(f"p4pt{r}", [128, 2 * 1024], BF16) as ptb,
        nc.sbuf_tensor(f"p4lrow{r}", [1, 3 * 1024], F32) as lrow,
        nc.sbuf_tensor(f"p4linv{r}", [128, 1024], F32) as linvb,
        nc.sbuf_tensor(f"p4ost{r}", [128, 2 * 1024], BF16) as ost,
        nc.psum_tensor(f"p4s0{r}", [128, 1024], F32) as s0,
        nc.psum_tensor(f"p4s1{r}", [128, 1024], F32) as s1,
        nc.psum_tensor(f"p4o{r}", [128, 1024], F32) as o_ps,
        nc.psum_tensor(f"p4l{r}", [1, 1024], F32) as l_ps,
    ):
        sps = [s0, s1]
        kv_ld = {}
        kv_freed = {}
        q_ld = {}
        exp_ev = {}
        mask_ev = {}
        pvl_ev = {}
        smm_ev = {}
        evac_ev = {}
        odma_ev = {}

        def load_kv(h, slot):
            SY.dma_start(out=kbuf[:, slot * S: slot * S + P2],
                         in_=T["kvk_d"].ap()[bass.ts(h, HD), 0:P2]
                         ).then_inc(s_kv[slot].h, 16)
            SY.dma_start(out=vbuf[:, slot * S: slot * S + P2].rearrange(
                             "p (c o) -> p c o", o=HD),
                         in_=T["kvv_d"].ap()[0:P2, bass.ts(h, HD)].rearrange(
                             "(c p) o -> p c o", p=128)
                         ).then_inc(s_kv[slot].h, 16)
            kv_ld[h] = s_kv[slot].inc(32)

        load_kv(0, 0)
        for kvh in range(HKV):
            slot = kvh % 2
            if kvh + 1 < HKV:
                if kvh >= 1:
                    SY.wait_ge(pe.h, kv_freed[kvh - 1])
                load_kv(kvh + 1, (kvh + 1) % 2)
            for qh in range(4):
                g = 4 * kvh + qh
                qslot = g % 2
                if g >= 2:
                    SY.wait_ge(pe.h, pvl_last[g - 2])
                SY.dma_start(out=qbuf[:, bass.ts(qslot, 1024)],
                             in_=T["q_d"].ap()[bass.ts(g, HD), :]
                             ).then_inc(s_q[qslot].h, 16)
                q_ld[g] = s_q[qslot].inc(16)
                if g == 0:
                    pvl_last = {}
                qt = qbuf[:, bass.ts(qslot, 1024)]
                TE.wait_ge(s_kv[slot].h, kv_ld[kvh])
                TE.wait_ge(s_q[qslot].h, q_ld[g])
                if g >= 1:
                    TE.wait_ge(ve.h, evac_ev[g - 1])
                for c in range(NKC):
                    spans, tris = spans_for(c, r)
                    sp = sps[c % 2]
                    kt = kbuf[:, slot * S + c * 128: slot * S + (c + 1) * 128]
                    vt = vbuf[:, slot * S + c * 128: slot * S + (c + 1) * 128]
                    key = (g, c)
                    if c >= 2:
                        TE.wait_ge(se.h, exp_ev[(g, c - 2)])
                    elif g >= 1 and c < 2:
                        TE.wait_ge(se.h, exp_last[g - 1])
                    for (a, b) in spans:
                        x = a
                        while x < b:
                            e = min(b, (x // 512 + 1) * 512)
                            ins = TE.matmul(sp[:, x:e], kt, qt[:, x:e],
                                            start=True, stop=True)
                            x = e
                    ins.then_inc(pe.h, 1)
                    smm_ev[key] = pe.inc()
                    pt = ptb[:, bass.ts(c % 2, 1024)]
                    SE.wait_ge(pe.h, smm_ev[key])
                    if c >= 2:
                        SE.wait_ge(pe.h, pvl_ev[(g, c - 2)])
                    for i, (a, b) in enumerate(spans):
                        ins = SE.activation(pt[:, a:b], sp[:, a:b], AF.Exp,
                                            bias=ebias[:, :])
                        if i == len(spans) - 1:
                            ins.then_inc(se.h, 1)
                    exp_ev[key] = se.inc()
                    if tris:
                        VE.wait_ge(se.h, exp_ev[key])
                        for o in tris:
                            VE.tensor_mul(pt[:, o:o + 128], pt[:, o:o + 128],
                                          tri_sb[:, :])
                        VE.engine_nop().then_inc(ve.h, 1)
                        mask_ev[key] = ve.inc()
                    TE.wait_ge(se.h, exp_ev[key])
                    if tris:
                        TE.wait_ge(ve.h, mask_ev[key])
                    first = (c == 0)
                    c_last_lo = 4 * r + 3
                    for (a, b) in spans:
                        x = a
                        while x < b:
                            e = min(b, (x // 512 + 1) * 512)
                            last = (c == (c_last_lo if x < 512 else NKC - 1))
                            TE.matmul(o_ps[:, x:e], vt, pt[:, x:e],
                                      start=first, stop=last)
                            ins = TE.matmul(l_ps[:, x:e], ones1b[:, :],
                                            pt[:, x:e], start=first, stop=last)
                            x = e
                    ins.then_inc(pe.h, 1)
                    pvl_ev[(g, c)] = pe.inc()
                exp_last = exp_last if g > 0 else {}
                exp_last[g] = exp_ev[(g, NKC - 1)]
                pvl_last[g] = pvl_ev[(g, NKC - 1)]
                if qh == 3:
                    kv_freed[kvh] = pvl_last[g]

                lr = lrow[:, 0:1024]
                li = lrow[:, 1024:2048]
                VE.wait_ge(pe.h, pvl_last[g])
                VE.wait_ge(s_ldbg.h, s_ldbg.n)
                VE.tensor_copy(lr, l_ps[:, :])
                VE.reciprocal(li, lr).then_inc(ve.h, 1)
                vv = ve.inc()
                TE.wait_ge(ve.h, vv)
                bcp = sps[0]
                TE.matmul(bcp[:, 0:512], ones1f[:, :], li[:, 0:512],
                          start=True, stop=True)
                TE.matmul(bcp[:, 512:1024], ones1f[:, :], li[:, 512:1024],
                          start=True, stop=True).then_inc(pe.h, 1)
                bc = pe.inc()
                VE.wait_ge(pe.h, bc)
                VE.tensor_copy(linvb[:, :], bcp[:, :])
                oo = ost[:, bass.ts(g % 2, 1024)]
                if g >= 2:
                    VE.wait_ge(s_ost[g % 2].h, odma_ev[g - 2])
                VE.tensor_mul(oo, o_ps[:, :], linvb[:, :]).then_inc(ve.h, 1)
                evac_ev[g] = ve.inc()
                SY.wait_ge(ve.h, evac_ev[g])
                SY.dma_start(out=T["o_d"].ap()[bass.ts(g, HD), :],
                             in_=oo).then_inc(s_ost[g % 2].h, 16)
                odma_ev[g] = s_ost[g % 2].inc(16)
                SY.dma_start(out=T["l_d"].ap()[g:g + 1, :], in_=li
                             ).then_inc(s_ldbg.h, 16)
                s_ldbg.inc(16)


def phase56(nc, r, T, env):
    TE, VE, SE, GP, SY = nc.tensor, nc.vector, nc.scalar, nc.gpsimd, nc.sync
    dg, dsy = env["dg"], env["dsy"]
    pe, ve, se = env["pe"], env["ve"], env["se"]
    lo0, hi0 = env["lo0"], env["hi0"]
    ones1b, ones1f, epst = env["ones1b"], env["ones1f"], env["epst"]
    onescf = env["onescf"]
    s_o = dsy[0]
    s_wo = [dg[0], dg[1]]
    s_x = [dsy[1], dsy[2]]
    s_xast = [dsy[3], dsy[4]]
    s_x2 = [dsy[5], dsy[6]]
    s_x2st = [dsy[7], dsy[8]]

    with (
        nc.sbuf_tensor(f"p5o{r}", [128, DC * 1024], BF16) as ob,
        nc.sbuf_tensor(f"p5wo{r}", [128, 2 * DC * HD], BF16) as wob,
        nc.sbuf_tensor(f"p5x{r}", [128, 2 * 1024], F32) as xb,
        nc.sbuf_tensor(f"p5xa{r}", [128, 2 * 1024], F32) as xab,
        nc.sbuf_tensor(f"p5sq{r}", [128, 2 * 1024], BF16) as sqb,
        nc.sbuf_tensor(f"p5rows{r}", [1, 6 * 1024], F32) as rows,
        nc.sbuf_tensor(f"p5ab{r}", [128, 2 * 1024], F32) as abb,
        nc.sbuf_tensor(f"p5xl2{r}", [128, 2 * 1024], BF16) as xl2b,
        nc.psum_tensor(f"p5w0{r}", [128, 1024], F32) as w0,
        nc.psum_tensor(f"p5w1{r}", [128, 1024], F32) as w1,
        nc.psum_tensor(f"p5sum{r}", [1, 1024], F32) as sum_ps,
        nc.psum_tensor(f"p5sq2{r}", [1, 1024], F32) as sq_ps,
    ):
        wps = [w0, w1]
        SY.dma_start(out=ob[:, :].rearrange("p (a w) -> p a w", a=DC),
                     in_=T["o_d"].ap().rearrange("(a p) w -> p a w", p=128)
                     ).then_inc(s_o.h, 16)
        o_ld = s_o.inc(16)
        TE.wait_ge(s_o.h, o_ld)

        xld_ev = {}
        wo_ld = {}
        mm_ev = {}
        ev_ev = {}
        sq_ev = {}
        st_ev = {}
        xa_dma = {}

        def load_x(dch, slot):
            x = xb[:, bass.ts(slot, 1024)]
            SY.dma_start(out=x[:, 0:512],
                         in_=T["xT"].ap()[bass.ts(dch, 128),
                                          lo0:lo0 + 512]).then_inc(s_x[slot].h, 16)
            SY.dma_start(out=x[:, 512:1024],
                         in_=T["xT"].ap()[bass.ts(dch, 128),
                                          hi0:hi0 + 512]).then_inc(s_x[slot].h, 16)
            xld_ev[dch] = s_x[slot].inc(32)

        def load_wo(dch, slot):
            GP.dma_start(out=wob[:, bass.ts(slot, DC * HD)].rearrange(
                             "p (c o) -> p c o", c=DC),
                         in_=T["wo"].ap()[:, bass.ts(dch, HD)].rearrange(
                             "(c p) o -> p c o", p=128)).then_inc(s_wo[slot].h, 16)
            wo_ld[dch] = s_wo[slot].inc(16)

        load_x(0, 0)
        load_wo(0, 0)
        for dch in range(DC):
            slot = dch % 2
            if dch + 1 < DC:
                if dch >= 1:
                    GP.wait_ge(pe.h, mm_ev[dch - 1])
                    SY.wait_ge(ve.h, ev_ev[dch - 1])
                load_x(dch + 1, (dch + 1) % 2)
                load_wo(dch + 1, (dch + 1) % 2)
            wp = wps[slot]
            TE.wait_ge(s_wo[slot].h, wo_ld[dch])
            if dch >= 2:
                TE.wait_ge(ve.h, ev_ev[dch - 2])
            for a in range(DC):
                for half in range(2):
                    ins = TE.matmul(wp[:, bass.ts(half, CH)],
                                    wob[:, slot * DC * HD + a * HD:
                                        slot * DC * HD + (a + 1) * HD],
                                    ob[:, a * 1024 + half * CH:
                                       a * 1024 + (half + 1) * CH],
                                    start=(a == 0), stop=(a == DC - 1))
            ins.then_inc(pe.h, 1)
            mm_ev[dch] = pe.inc()
            xa = xab[:, bass.ts(slot, 1024)]
            VE.wait_ge(pe.h, mm_ev[dch])
            VE.wait_ge(s_x[slot].h, xld_ev[dch])
            if dch >= 2:
                VE.wait_ge(s_xast[slot].h, xa_dma[dch - 2])
                VE.wait_ge(pe.h, st_ev[dch - 2])
            VE.tensor_add(xa, wp[:, :], xb[:, bass.ts(slot, 1024)]
                          ).then_inc(ve.h, 1)
            ev_ev[dch] = ve.inc()
            SY.wait_ge(ve.h, ev_ev[dch])
            SY.dma_start(out=T["xattn_d"].ap()[bass.ts(dch, 128), :],
                         in_=xa).then_inc(s_xast[slot].h, 16)
            xa_dma[dch] = s_xast[slot].inc(16)
            sq = sqb[:, bass.ts(slot, 1024)]
            SE.wait_ge(ve.h, ev_ev[dch])
            if dch >= 2:
                SE.wait_ge(pe.h, st_ev[dch - 2])
            SE.square(sq, xa).then_inc(se.h, 1)
            sq_ev[dch] = se.inc()
            TE.wait_ge(se.h, sq_ev[dch])
            for half in range(2):
                hs = bass.ts(half, CH)
                TE.matmul(sum_ps[:, hs], onescf[:, :], xa[:, hs],
                          start=(dch == 0), stop=(dch == DC - 1))
                ins = TE.matmul(sq_ps[:, hs], ones1b[:, :], sq[:, hs],
                                start=(dch == 0), stop=(dch == DC - 1))
            ins.then_inc(pe.h, 1)
            st_ev[dch] = pe.inc()

        mean = rows[:, 0:1024]
        ex2 = rows[:, 1024:2048]
        var = rows[:, 2048:3072]
        sqv = rows[:, 3072:4096]
        a_row = rows[:, 4096:5120]
        b_row = rows[:, 5120:6144]
        VE.wait_ge(pe.h, st_ev[DC - 1])
        VE.tensor_scalar_mul(mean, sum_ps[:, :], 1.0 / D)
        VE.tensor_scalar_mul(ex2, sq_ps[:, :], 1.0 / D)
        VE.tensor_mul(var, mean, mean)
        VE.tensor_sub(var, ex2, var).then_inc(ve.h, 1)
        vv = ve.inc()
        SE.wait_ge(ve.h, vv)
        SE.activation(sqv, var, AF.Sqrt, bias=epst[:, :]).then_inc(se.h, 1)
        sv = se.inc()
        VE.wait_ge(se.h, sv)
        VE.reciprocal(a_row, sqv)
        VE.tensor_mul(b_row, mean, a_row).then_inc(ve.h, 1)
        rows_done = ve.inc()
        TE.wait_ge(ve.h, rows_done)
        for half in range(2):
            hs = bass.ts(half, CH)
            TE.matmul(w0[:, hs], ones1f[:, :], a_row[:, hs],
                      start=True, stop=True)
            ins = TE.matmul(w1[:, hs], ones1f[:, :], b_row[:, hs],
                            start=True, stop=True)
        ins.then_inc(pe.h, 1)
        bc = pe.inc()
        VE.wait_ge(pe.h, bc)
        VE.tensor_copy(abb[:, 0:1024], w0[:, :])
        VE.tensor_copy(abb[:, 1024:2048], w1[:, :]).then_inc(ve.h, 1)
        ab_ok = ve.inc()

        x2ld = {}
        x2ev = {}
        SY.wait_ge(pe.h, st_ev[DC - 1])
        for dch in range(DC):
            slot = dch % 2
            xa = xab[:, bass.ts(slot, 1024)]
            if dch >= 2:
                SY.wait_ge(ve.h, x2ev[dch - 2])
            SY.dma_start(out=xa, in_=T["xattn_d"].ap()[bass.ts(dch, 128), :]
                         ).then_inc(s_x2[slot].h, 16)
            x2ld[dch] = s_x2[slot].inc(16)
            xl2 = xl2b[:, bass.ts(slot, 1024)]
            VE.wait_ge(s_x2[slot].h, x2ld[dch])
            VE.wait_ge(ve.h, ab_ok)
            if dch >= 2:
                VE.wait_ge(s_x2st[slot].h, x2st[dch - 2])
            VE.tensor_mul(xa, xa, abb[:, 0:1024])
            VE.tensor_sub(xl2, xa, abb[:, 1024:2048]).then_inc(ve.h, 1)
            x2ev[dch] = ve.inc()
            if dch == 0:
                x2st = {}
            SY.wait_ge(ve.h, x2ev[dch])
            SY.dma_start(out=T["xln2_d"].ap()[bass.ts(dch, 128), :],
                         in_=xl2).then_inc(s_x2st[slot].h, 16)
            x2st[dch] = s_x2st[slot].inc(16)


def phase7(nc, r, T, env):
    TE, VE, SE, GP, SY = nc.tensor, nc.vector, nc.scalar, nc.gpsimd, nc.sync
    dg, dsy = env["dg"], env["dsy"]
    pe, ve, se = env["pe"], env["ve"], env["se"]
    s_lora, s_xl = dg[0], dsy[0]
    s_gu = [dg[1], dg[2]]
    s_dn = [dg[3], dg[4]]
    s_xa = [dsy[1], dsy[2]]
    s_out = [dsy[3], dsy[4]]

    with (
        nc.sbuf_tensor(f"p7hid{r}", [128, HC * CH], BF16) as hid,
        nc.sbuf_tensor(f"p7shr{r}", [128, 2 * HC * HD], BF16) as shr,
        nc.sbuf_tensor(f"p7gu{r}", [128, 2 * 2 * DC * HD], BF16) as gub,
        nc.sbuf_tensor(f"p7ab{r}", [128, DC * RK * 2], BF16) as lab,
        nc.sbuf_tensor(f"p7da{r}", [128, HC * RK], BF16) as dab,
        nc.sbuf_tensor(f"p7bgu{r}", [RK, 2 * 2 * HD], BF16) as bgu,
        nc.sbuf_tensor(f"p7db{r}", [RK, D], BF16) as dbb,
        nc.sbuf_tensor(f"p7z{r}", [RK, 3 * CH], BF16) as zb,
        nc.sbuf_tensor(f"p7sil{r}", [128, 2 * CH], BF16) as sil,
        nc.sbuf_tensor(f"p7xa{r}", [128, 2 * CH], F32) as xab,
        nc.sbuf_tensor(f"p7out{r}", [128, 2 * CH], F32) as outb,
        nc.psum_tensor(f"p7g0{r}", [128, CH], F32) as g0,
        nc.psum_tensor(f"p7g1{r}", [128, CH], F32) as g1,
        nc.psum_tensor(f"p7u0{r}", [128, CH], F32) as u0,
        nc.psum_tensor(f"p7u1{r}", [128, CH], F32) as u1,
        nc.psum_tensor(f"p7zps{r}", [RK, 2 * CH], F32) as z_ps,
        nc.psum_tensor(f"p7zd{r}", [RK, CH], F32) as zd_ps,
        nc.psum_tensor(f"p7d0{r}", [128, CH], F32) as d0,
    ):
        xl2 = shr[:, 0:DC * CH]
        dwb = shr
        gps, ups = [g0, g1], [u0, u1]
        GP.dma_start(out=lab[:, 0:DC * RK].rearrange("p (c k) -> p c k", c=DC),
                     in_=T["gaT"].ap().rearrange("(c p) k -> p c k", p=128)
                     ).then_inc(s_lora.h, 16)
        GP.dma_start(out=lab[:, DC * RK:].rearrange("p (c k) -> p c k", c=DC),
                     in_=T["uaT"].ap().rearrange("(c p) k -> p c k", p=128)
                     ).then_inc(s_lora.h, 16)
        GP.dma_start(out=dab[:, :].rearrange("p (c k) -> p c k", c=HC),
                     in_=T["daT"].ap().rearrange("(c p) k -> p c k", p=128)
                     ).then_inc(s_lora.h, 16)
        GP.dma_start(out=dbb[:, :], in_=T["dbT"].ap()).then_inc(s_lora.h, 16)
        lora_ld = s_lora.inc(64)
        TE.wait_ge(s_lora.h, lora_ld)

        prev_tile_pe = None
        for tile in range(2):
            tc = bass.ts(tile, CH)
            if prev_tile_pe is not None:
                SY.wait_ge(pe.h, prev_tile_pe)
            SY.dma_start(out=xl2.rearrange("p (c w) -> p c w", c=DC),
                         in_=T["xln2_d"].ap()[:, tc].rearrange(
                             "(c p) w -> p c w", p=128)).then_inc(s_xl.h, 16)
            xl_ld = s_xl.inc(16)
            TE.wait_ge(s_xl.h, xl_ld)
            if tile == 1:
                TE.wait_ge(ve.h, z_ev)
                VE.wait_ge(pe.h, prev_tile_pe)
            for c in range(DC):
                xc = xl2[:, bass.ts(c, CH)]
                TE.matmul(z_ps[:, 0:CH], lab[:, c * RK:(c + 1) * RK], xc,
                          start=(c == 0), stop=(c == DC - 1))
                ins = TE.matmul(z_ps[:, CH:2 * CH],
                                lab[:, DC * RK + c * RK:
                                    DC * RK + (c + 1) * RK],
                                xc, start=(c == 0), stop=(c == DC - 1))
            ins.then_inc(pe.h, 1)
            zmm = pe.inc()
            VE.wait_ge(pe.h, zmm)
            VE.tensor_copy(zb[:, 0:2 * CH], z_ps[:, :]).then_inc(ve.h, 1)
            z_ev = ve.inc()

            gu_ld = {}
            gumm = {}
            hid_ev = {}
            for h in range(HC):
                slot = h % 2
                if h >= 2:
                    GP.wait_ge(pe.h, gumm[h - 2])
                GP.dma_start(
                    out=gub[:, slot * 2 * DC * HD: slot * 2 * DC * HD + DC * HD
                            ].rearrange("p (c o) -> p c o", c=DC),
                    in_=T["gwT"].ap()[:, bass.ts(h, HD)].rearrange(
                        "(c p) o -> p c o", p=128)).then_inc(s_gu[slot].h, 16)
                GP.dma_start(
                    out=gub[:, slot * 2 * DC * HD + DC * HD:
                            (slot + 1) * 2 * DC * HD
                            ].rearrange("p (c o) -> p c o", c=DC),
                    in_=T["uwT"].ap()[:, bass.ts(h, HD)].rearrange(
                        "(c p) o -> p c o", p=128)).then_inc(s_gu[slot].h, 16)
                GP.dma_start(out=bgu[:, bass.ts(slot, 2 * HD)],
                             in_=T["gubT"].ap()[:, bass.ts(h, 2 * HD)]
                             ).then_inc(s_gu[slot].h, 16)
                gu_ld[h] = s_gu[slot].inc(48)
                gp_, up_ = gps[slot], ups[slot]
                TE.wait_ge(s_gu[slot].h, gu_ld[h])
                TE.wait_ge(ve.h, z_ev)
                if h >= 2:
                    TE.wait_ge(ve.h, hid_ev[h - 2])
                gbase = slot * 2 * DC * HD
                for c in range(DC):
                    xc = xl2[:, bass.ts(c, CH)]
                    TE.matmul(gp_[:, :], gub[:, gbase + c * HD:
                                             gbase + (c + 1) * HD],
                              xc, start=(c == 0), stop=False)
                    TE.matmul(up_[:, :], gub[:, gbase + DC * HD + c * HD:
                                             gbase + DC * HD + (c + 1) * HD],
                              xc, start=(c == 0), stop=False)
                TE.matmul(gp_[:, :], bgu[:, slot * 2 * HD: slot * 2 * HD + HD],
                          zb[:, 0:CH], start=False, stop=True)
                ins = TE.matmul(up_[:, :], bgu[:, slot * 2 * HD + HD:
                                               (slot + 1) * 2 * HD],
                                zb[:, CH:2 * CH], start=False, stop=True)
                ins.then_inc(pe.h, 1)
                gumm[h] = pe.inc()
                sl = sil[:, bass.ts(slot, CH)]
                SE.wait_ge(pe.h, gumm[h])
                if h >= 2:
                    SE.wait_ge(ve.h, hid_ev[h - 2])
                SE.activation(sl, gp_[:, :], AF.Silu).then_inc(se.h, 1)
                sil_ev = se.inc()
                hc_ = hid[:, bass.ts(h, CH)]
                VE.wait_ge(se.h, sil_ev)
                VE.tensor_mul(hc_, sl, up_[:, :]).then_inc(ve.h, 1)
                hid_ev[h] = ve.inc()
                TE.wait_ge(ve.h, hid_ev[h])
                if tile == 1 and h == 0:
                    TE.wait_ge(ve.h, zd_ev)
                ins = TE.matmul(zd_ps[:, :], dab[:, h * RK:(h + 1) * RK], hc_,
                                start=(h == 0), stop=(h == HC - 1))
                if h == HC - 1:
                    ins.then_inc(pe.h, 1)
                    zd_mm = pe.inc()
            VE.wait_ge(pe.h, zd_mm)
            VE.tensor_copy(zb[:, 2 * CH:3 * CH], zd_ps[:, :]).then_inc(ve.h, 1)
            zd_ev = ve.inc()

            dn_ld = {}
            dnmm = {}
            out_ev = {}
            xa_ld = {}
            ost_ev = {}
            GP.wait_ge(pe.h, gumm[HC - 1])
            for dch in range(DC):
                slot = dch % 2
                if dch >= 2:
                    GP.wait_ge(pe.h, dnmm[dch - 2])
                GP.dma_start(out=dwb[:, bass.ts(slot, HC * HD)].rearrange(
                                 "p (c o) -> p c o", c=HC),
                             in_=T["dwT"].ap()[:, bass.ts(dch, HD)].rearrange(
                                 "(c p) o -> p c o", p=128)
                             ).then_inc(s_dn[slot].h, 16)
                dn_ld[dch] = s_dn[slot].inc(16)
                xa = xab[:, bass.ts(slot, CH)]
                if dch >= 2:
                    SY.wait_ge(ve.h, out_ev[dch - 2])
                SY.dma_start(out=xa,
                             in_=T["xattn_d"].ap()[bass.ts(dch, 128), tc]
                             ).then_inc(s_xa[slot].h, 16)
                xa_ld[dch] = s_xa[slot].inc(16)
                TE.wait_ge(s_dn[slot].h, dn_ld[dch])
                TE.wait_ge(ve.h, zd_ev)
                if dch >= 1:
                    TE.wait_ge(ve.h, out_ev[dch - 1])
                dbase = slot * HC * HD
                for hcc in range(HC):
                    TE.matmul(d0[:, :], dwb[:, dbase + hcc * HD:
                                            dbase + (hcc + 1) * HD],
                              hid[:, bass.ts(hcc, CH)],
                              start=(hcc == 0), stop=False)
                ins = TE.matmul(d0[:, :], dbb[:, bass.ts(dch, HD)],
                                zb[:, 2 * CH:3 * CH], start=False, stop=True)
                ins.then_inc(pe.h, 1)
                dnmm[dch] = pe.inc()
                oo = outb[:, bass.ts(slot, CH)]
                VE.wait_ge(pe.h, dnmm[dch])
                VE.wait_ge(s_xa[slot].h, xa_ld[dch])
                if dch >= 2:
                    VE.wait_ge(s_out[slot].h, ost_ev[dch - 2])
                VE.tensor_add(oo, d0[:, :], xa).then_inc(ve.h, 1)
                out_ev[dch] = ve.inc()
                SY.wait_ge(ve.h, out_ev[dch])
                SY.dma_start(out=T["out_fm"].ap()[bass.ts(dch, 128), tc],
                             in_=oo).then_inc(s_out[slot].h, 16)
                ost_ev[dch] = s_out[slot].inc(16)
            prev_tile_pe = dnmm[DC - 1]


def _rsw_mat():
    m = np.zeros((128, 128), dtype=np.float32)
    for d in range(64):
        m[d + 64, d] = -1.0
        m[d, d + 64] = 1.0
    return m.astype(ml_dtypes.bfloat16)


def _rope_tables():
    inv = 1.0 / (10000.0 ** (np.arange(0, HD, 2, dtype=np.float64) / HD))
    t = np.arange(S, dtype=np.float64)
    fr = np.outer(t, inv)
    emb = np.concatenate([fr, fr], axis=1)
    return (np.cos(emb).astype(np.float32).T.copy(),
            np.sin(emb).astype(np.float32).T.copy())


def kernel(L0_attn_hidden_states, L0_attn_Wq, L0_attn_Wk, L0_attn_Wv,
           L0_attn_Wo, L0_ffn_gate_w, L0_ffn_up_w, L0_ffn_down_w,
           L0_ffn_gate_a, L0_ffn_gate_b, L0_ffn_up_a, L0_ffn_up_b,
           L0_ffn_down_a, L0_ffn_down_b):
    x = np.asarray(L0_attn_hidden_states, dtype=np.float32).reshape(S, D)
    xT = np.ascontiguousarray(x.T)
    f32 = lambda a: np.ascontiguousarray(np.asarray(a, dtype=np.float32))
    wq, wk, wv, wo = f32(L0_attn_Wq), f32(L0_attn_Wk), f32(L0_attn_Wv), f32(L0_attn_Wo)
    gwT = np.ascontiguousarray(f32(L0_ffn_gate_w).T)
    uwT = np.ascontiguousarray(f32(L0_ffn_up_w).T)
    dwT = np.ascontiguousarray(f32(L0_ffn_down_w).T)
    gaT = np.ascontiguousarray(f32(L0_ffn_gate_a).T)
    uaT = np.ascontiguousarray(f32(L0_ffn_up_a).T)
    daT = np.ascontiguousarray(f32(L0_ffn_down_a).T)
    gbT = np.ascontiguousarray(f32(L0_ffn_gate_b).T)
    ubT = np.ascontiguousarray(f32(L0_ffn_up_b).T)
    dbT = np.ascontiguousarray(f32(L0_ffn_down_b).T)
    gub = np.stack([gbT.reshape(RK, HC, HD), ubT.reshape(RK, HC, HD)],
                   axis=2).reshape(RK, 2 * HID)
    gub = np.ascontiguousarray(gub)

    cosT, sinT = _rope_tables()
    scale = 1.0 / np.sqrt(np.float32(HD))
    tri = np.triu(np.ones((128, 128), dtype=np.float32))
    tri = tri.astype(ml_dtypes.bfloat16)
    rsw = _rsw_mat()

    nc = build_program(debug=False)

    shared = dict(xT=xT, wq=wq, wk=wk, wv=wv, wo=wo, gwT=gwT, uwT=uwT,
                  dwT=dwT, gaT=gaT, uaT=uaT, daT=daT, gubT=gub, dbT=dbT,
                  cosk=cosT, sink=sinT, tri=tri, rsw=rsw)
    in_maps = []
    for r in range(NCORES):
        lo, hi = rank_chunks(r)
        cols = np.r_[lo * CH:(lo + 1) * CH, hi * CH:(hi + 1) * CH]
        m = dict(shared)
        m["cosq"] = np.ascontiguousarray(cosT[:, cols] * scale)
        m["sinq"] = np.ascontiguousarray(sinT[:, cols] * scale)
        in_maps.append(m)

    res = run_bass_kernel_spmd(nc, in_maps, core_ids=list(range(NCORES)))

    out = np.empty((S, D), dtype=np.float32)
    for r in range(NCORES):
        lo, hi = rank_chunks(r)
        of = res.results[r]["out_fm"]
        out[lo * CH:(lo + 1) * CH] = of[:, 0:CH].T
        out[hi * CH:(hi + 1) * CH] = of[:, CH:1024].T
    return out.reshape(1, S, D)


# version 02
# speedup vs baseline: 1.2780x; 1.2780x over previous
import numpy as np
import ml_dtypes
import concourse.bass as bass
import concourse.mybir as mybir
from concourse.bass_utils import run_bass_kernel_spmd

F32 = mybir.dt.float32
BF16 = mybir.dt.bfloat16
AF = mybir.ActivationFunctionType
ALU = mybir.AluOpType

D = 4096
S = 8192
HKV = 8
HQ = 32
HD = 128
HID = 11008
RK = 16
CH = 512
NCH = S // CH
DC = D // 128
HC = HID // 128
EPS = 1e-5
EXP_SHIFT = -12.0
NCORES = 8
STOP_AFTER = 7
P1_TILES = None
P1_NO_K = False
P1_NO_XLN = False
P1_NO_ROPE = False
P1_NO_KST = False


class SemC:

    def __init__(self, h):
        self.h = h
        self.n = 0

    def inc(self, k=1):
        self.n += k
        return self.n


NDMA_G = 5
NDMA_S = 9


def rank_chunks(r):
    return r, 15 - r


def build_program(debug=False):
    nc = bass.Bass("TRN2", target_bir_lowering=False,
                   detect_race_conditions=False)

    t_in = {}

    def din(name, shape, dt=F32):
        t_in[name] = nc.dram_tensor(name, shape, dt, kind="ExternalInput")
        return t_in[name]

    xT = din("xT", [D, S])
    wq = din("wq", [D, D])
    wk = din("wk", [D, HKV * HD])
    wv = din("wv", [D, HKV * HD])
    wo = din("wo", [D, D])
    gwT = din("gwT", [D, HID])
    uwT = din("uwT", [D, HID])
    dwT = din("dwT", [HID, D])
    gaT = din("gaT", [D, RK])
    uaT = din("uaT", [D, RK])
    daT = din("daT", [HID, RK])
    gubT = din("gubT", [RK, 2 * HID])
    dbT = din("dbT", [RK, D])
    cosq = din("cosq", [HD, 1024])
    sinq = din("sinq", [HD, 1024])
    cosk = din("cosk", [HD, S])
    sink = din("sink", [HD, S])
    tri = din("tri", [128, 128], BF16)
    rsw = din("rsw", [128, 128], BF16)

    out_fm = nc.dram_tensor("out_fm", [D, 1024], F32, kind="ExternalOutput")

    dbg_kind = "ExternalOutput" if debug else "Internal"
    xln_d = nc.dram_tensor("xln_d", [D, S], BF16, kind=dbg_kind)
    kvk_d = nc.dram_tensor("kvk_d", [HKV * HD, S], BF16, kind=dbg_kind)
    kvv_d = nc.dram_tensor("kvv_d", [S, HKV * HD], BF16, kind=dbg_kind)
    q_d = nc.dram_tensor("q_d", [D, 1024], BF16, kind=dbg_kind)
    o_d = nc.dram_tensor("o_d", [D, 1024], BF16, kind=dbg_kind)
    l_d = nc.dram_tensor("l_d", [HQ, 1024], F32, kind=dbg_kind)
    xattn_d = nc.dram_tensor("xattn_d", [D, 1024], F32, kind=dbg_kind)
    xln2_d = nc.dram_tensor("xln2_d", [D, 1024], BF16, kind=dbg_kind)

    tens = dict(xT=xT, wq=wq, wk=wk, wv=wv, wo=wo, gwT=gwT, uwT=uwT, dwT=dwT,
                gaT=gaT, uaT=uaT, daT=daT, gubT=gubT, dbT=dbT,
                cosq=cosq, sinq=sinq, cosk=cosk, sink=sink, tri=tri, rsw=rsw,
                out_fm=out_fm, xln_d=xln_d, kvk_d=kvk_d, kvv_d=kvv_d,
                q_d=q_d, o_d=o_d, l_d=l_d, xattn_d=xattn_d, xln2_d=xln2_d)

    engines = [nc.tensor, nc.vector, nc.scalar, nc.gpsimd, nc.sync]
    pid = nc.partition_id(engines=[e.engine for e in engines])

    sems = dict(
        pe=nc.alloc_semaphore("s_pe"),
        ve=nc.alloc_semaphore("s_ve"),
        se=nc.alloc_semaphore("s_se"),
        bar=[nc.alloc_semaphore(f"s_bar{i}") for i in range(5)],
        dg=[nc.alloc_semaphore(f"s_dg{i}") for i in range(NDMA_G)],
        dsy=[nc.alloc_semaphore(f"s_dsy{i}") for i in range(NDMA_S)],
    )

    for r in nc.Switch(engines=engines, index=[pid] * 5, n=8):
        emit_rank(nc, r, tens, sems)

    return nc


def emit_rank(nc, r, T, sems):
    lo, hi = rank_chunks(r)
    NT = 16 - r
    P2 = NT * CH
    lo0, hi0 = lo * CH, hi * CH

    TE, VE, SE, GP, SY = nc.tensor, nc.vector, nc.scalar, nc.gpsimd, nc.sync

    pe = SemC(sems["pe"])
    ve = SemC(sems["ve"])
    se = SemC(sems["se"])
    dg = [SemC(h) for h in sems["dg"]]
    dsy = [SemC(h) for h in sems["dsy"]]
    bar = [SemC(h) for h in sems["bar"]]
    _bk = [0]

    def barrier():
        _bk[0] += 1
        k = _bk[0]
        TE.nop().then_inc(bar[0].h, 1)
        VE.memset(dummy_v[0:1, 0:1], 0.0).then_inc(bar[1].h, 1)
        SE.memzero(dummy_s[0:1, 0:1]).then_inc(bar[2].h, 1)
        for d in dg:
            GP.wait_ge(d.h, d.n)
        for d in dsy:
            SY.wait_ge(d.h, d.n)
        GP.nop().then_inc(bar[3].h, 1)
        SY.nop().then_inc(bar[4].h, 1)
        for i, e in enumerate([TE, VE, SE, GP, SY]):
            for j in range(5):
                if j != i:
                    e.wait_ge(bar[j].h, k)

    xTap, wqap, wkap, wvap, woap = (T[k].ap() for k in
                                    ("xT", "wq", "wk", "wv", "wo"))

    with (
        nc.sbuf_tensor(f"dummy{r}", [1, 8], F32) as dummy_v,
        nc.sbuf_tensor(f"dummys{r}", [1, 8], F32) as dummy_s,
        nc.sbuf_tensor(f"ones1b{r}", [128, 1], BF16) as ones1b,
        nc.sbuf_tensor(f"ones1f{r}", [1, 128], F32) as ones1f,
        nc.sbuf_tensor(f"onescf{r}", [128, 1], F32) as onescf,
        nc.sbuf_tensor(f"ebias{r}", [128, 1], F32) as ebias,
        nc.sbuf_tensor(f"epst{r}", [1, 1], F32) as epst,
        nc.sbuf_tensor(f"tri{r}", [128, 128], BF16) as tri_sb,
        nc.sbuf_tensor(f"rsw{r}", [128, 128], BF16) as rsw_sb,
    ):
        VE.memset(ones1b[:, :], 1.0)
        VE.memset(ones1f[:, :], 1.0)
        VE.memset(onescf[:, :], 1.0)
        VE.memset(ebias[:, :], EXP_SHIFT)
        VE.memset(epst[:, :], EPS).then_inc(ve.h, 1)
        ve.inc()
        SY.dma_start(out=tri_sb[:, :], in_=T["tri"].ap()).then_inc(dsy[8].h, 16)
        SY.dma_start(out=rsw_sb[:, :], in_=T["rsw"].ap()).then_inc(dsy[8].h, 16)
        dsy[8].inc(32)
        TE.wait_ge(dsy[8].h, dsy[8].n)
        TE.wait_ge(ve.h, ve.n)
        SE.wait_ge(ve.h, ve.n)
        VE.wait_ge(dsy[8].h, dsy[8].n)

        if STOP_AFTER >= 1:
            phase1(nc, r, T, locals())
            barrier()
        if STOP_AFTER >= 2:
            phase2(nc, r, T, locals())
            barrier()
        if STOP_AFTER >= 3:
            phase3(nc, r, T, locals())
            barrier()
        if STOP_AFTER >= 4:
            phase4(nc, r, T, locals())
            barrier()
        if STOP_AFTER >= 5:
            phase56(nc, r, T, locals())
            barrier()
        if STOP_AFTER >= 7:
            phase7(nc, r, T, locals())

        for d in dg:
            GP.wait_ge(d.h, d.n)
        for d in dsy:
            SY.wait_ge(d.h, d.n)


def phase1(nc, r, T, env):
    TE, VE, SE, GP, SY = nc.tensor, nc.vector, nc.scalar, nc.gpsimd, nc.sync
    dg, dsy = env["dg"], env["dsy"]
    pe, ve, se = env["pe"], env["ve"], env["se"]
    ones1b, ones1f, epst = env["ones1b"], env["ones1f"], env["epst"]
    NT = env["NT"]
    s_wk, s_xt = dg[0], dg[1]
    s_rope = [dsy[0], dsy[1]]
    s_xlnst = [dsy[2], dsy[3]]
    s_kst = [dsy[4], dsy[5]]

    with (
        nc.sbuf_tensor(f"p1wk{r}", [128, DC * HKV * HD], BF16) as wk_sb,
        nc.sbuf_tensor(f"p1xt{r}", [128, DC * CH], BF16) as xt,
        nc.sbuf_tensor(f"p1xsq{r}", [128, 4 * CH], BF16) as xsq,
        nc.sbuf_tensor(f"p1xln{r}", [128, 2 * DC * CH], BF16) as xlnb,
        nc.sbuf_tensor(f"p1tmp{r}", [128, CH], BF16) as tmp,
        nc.sbuf_tensor(f"p1tmpf{r}", [128, 2 * CH], F32) as tmpf,
        nc.sbuf_tensor(f"p1kc{r}", [128, CH], BF16) as kc_sb,
        nc.sbuf_tensor(f"p1rows{r}", [1, 6 * CH], F32) as rows,
        nc.sbuf_tensor(f"p1rope{r}", [128, 2 * CH], F32) as ropeb,
        nc.sbuf_tensor(f"p1kst{r}", [128, 2 * HKV * CH], BF16) as kst,
        nc.psum_tensor(f"p1sum{r}", [1, CH], F32) as sum_ps,
        nc.psum_tensor(f"p1sq{r}", [1, CH], F32) as sq_ps,
        nc.psum_tensor(f"p1ab{r}", [128, CH], F32) as ab_ps,
        nc.psum_tensor(f"p1bb{r}", [128, CH], F32) as bb_ps,
        nc.psum_tensor(f"p1k0{r}", [128, CH], F32) as k0_ps,
        nc.psum_tensor(f"p1k1{r}", [128, CH], F32) as k1_ps,
        nc.psum_tensor(f"p1kr{r}", [128, CH], F32) as kr_ps,
    ):
        kps = [k0_ps, k1_ps]
        rsw_sb = env["rsw_sb"]
        GP.dma_start(out=wk_sb[:, :].rearrange("p (c o) -> p c o", c=DC),
                     in_=T["wk"].ap().rearrange("(c p) o -> p c o", p=128)
                     ).then_inc(s_wk.h, 16)
        s_wk.inc(16)
        TE.wait_ge(s_wk.h, s_wk.n)

        xln_done = [None] * NT
        kdma_done = [None] * NT
        rope_done = [None] * NT
        stats_read = [None] * NT
        xlnst_ev = [None] * NT

        sq_cnt = 0

        NTl = min(NT, P1_TILES) if P1_TILES else NT
        for t in range(NTl):
            tc = bass.ts(t, CH)
            if t > 0:
                GP.wait_ge(ve.h, xln_done[t - 1])
            GP.dma_start(out=xt[:, :].rearrange("p (c w) -> p c w", c=DC),
                         in_=xTap_slice(T, tc)).then_inc(s_xt.h, 16)
            s_xt.inc(16)
            ld = s_xt.n

            SE.wait_ge(s_xt.h, ld)
            TE.wait_ge(s_xt.h, ld)
            if t > 0:
                TE.wait_ge(ve.h, stats_read[t - 1])
            for c in range(DC):
                xc = xt[:, bass.ts(c, CH)]
                xq = xsq[:, bass.ts(c % 4, CH)]
                if c >= 4:
                    SE.wait_ge(pe.h, mmsq_base + (c - 4) + 1)
                SE.square(xq, xc).then_inc(se.h, 1)
                sq_cnt = se.inc()
                TE.matmul(sum_ps[:, :], ones1b[:, :], xc,
                          start=(c == 0), stop=(c == DC - 1))
                TE.wait_ge(se.h, sq_cnt)
                if c == 0:
                    mmsq_base = pe.n
                TE.matmul(sq_ps[:, :], ones1b[:, :], xq,
                          start=(c == 0), stop=(c == DC - 1)).then_inc(pe.h, 1)
                pe.inc()
            stats_done = pe.n

            mean = rows[:, 0:CH]
            ex2 = rows[:, CH:2 * CH]
            var = rows[:, 2 * CH:3 * CH]
            sqv = rows[:, 3 * CH:4 * CH]
            a_row = rows[:, 4 * CH:5 * CH]
            b_row = rows[:, 5 * CH:6 * CH]
            VE.wait_ge(pe.h, stats_done)
            VE.tensor_scalar_mul(mean, sum_ps[:, :], 1.0 / D)
            VE.tensor_scalar_mul(ex2, sq_ps[:, :], 1.0 / D)
            VE.engine_nop().then_inc(ve.h, 1)
            stats_read[t] = ve.inc()
            VE.tensor_mul(var, mean, mean)
            VE.tensor_sub(var, ex2, var).then_inc(ve.h, 1)
            vv = ve.inc()
            SE.wait_ge(ve.h, vv)
            SE.activation(sqv, var, AF.Sqrt, bias=epst[:, :]).then_inc(se.h, 1)
            sv = se.inc()
            VE.wait_ge(se.h, sv)
            VE.reciprocal(a_row, sqv)
            VE.tensor_mul(b_row, mean, a_row).then_inc(ve.h, 1)
            rows_done = ve.inc()

            TE.wait_ge(ve.h, rows_done)
            if t > 0:
                TE.wait_ge(ve.h, xln_done[t - 1])
            TE.matmul(ab_ps[:, :], ones1f[:, :], a_row, start=True, stop=True)
            TE.matmul(bb_ps[:, :], ones1f[:, :], b_row,
                      start=True, stop=True).then_inc(pe.h, 1)
            bcast_done = pe.inc()

            if P1_NO_XLN:
                VE.engine_nop().then_inc(ve.h, 1)
                xln_done[t] = ve.inc()
                VE.engine_nop().then_inc(ve.h, 1)
                rope_done[t] = ve.inc()
                kproj_done[t] = pe.n if t > 0 else pe.n
                if t == 0:
                    kproj_done = [pe.n] * NT
                continue
            xl = xlnb[:, bass.ts(t % 2, DC * CH)]
            VE.wait_ge(pe.h, bcast_done)
            if t > 1:
                VE.wait_ge(pe.h, kproj_done[t - 2])
                VE.wait_ge(s_xlnst[t % 2].h, xlnst_ev[t - 2])
            for c in range(DC):
                xc = xt[:, bass.ts(c, CH)]
                VE.tensor_mul(tmp[:, :], xc, ab_ps[:, :])
                VE.tensor_sub(xl[:, bass.ts(c, CH)], tmp[:, :], bb_ps[:, :])
            VE.engine_nop().then_inc(ve.h, 1)
            xln_done[t] = ve.inc()

            SY.wait_ge(ve.h, xln_done[t])
            SY.dma_start(out=T["xln_d"].ap()[:, tc].rearrange(
                             "(c p) w -> p c w", p=128),
                         in_=xl.rearrange("p (c w) -> p c w", c=DC)
                         ).then_inc(s_xlnst[t % 2].h, 16)
            xlnst_ev[t] = s_xlnst[t % 2].inc(16)

            rope_cos = ropeb[:, 0:CH]
            rope_sin = ropeb[:, CH:2 * CH]
            if t > 0:
                SY.wait_ge(ve.h, rope_done[t - 1])
            sr = s_rope[0]
            SY.dma_start(out=rope_cos, in_=T["cosk"].ap()[:, tc]).then_inc(sr.h, 16)
            SY.dma_start(out=rope_sin, in_=T["sink"].ap()[:, tc]).then_inc(sr.h, 16)
            sr.inc(32)
            rope_ld = sr.n

            if t == 0:
                kproj_done = [None] * NT
            if P1_NO_K:
                VE.engine_nop().then_inc(ve.h, 1)
                rope_done[t] = ve.inc()
                kproj_done[t] = pe.n
                continue
            ks = kst[:, bass.ts(t % 2, HKV * CH)]
            if t > 1:
                VE.wait_ge(s_kst[t % 2].h, kdma_done[t - 2])
            if t > 0:
                TE.wait_ge(ve.h, rope_done[t - 1])
            TE.wait_ge(ve.h, xln_done[t])
            kc_ev = {}
            rot_mm = {}
            rope_ev = {}
            for h in range(HKV):
                kp = kps[h % 2]
                if h >= 2:
                    TE.wait_ge(ve.h, kc_ev[h - 2])
                for c in range(DC):
                    ins = TE.matmul(kp[:, :],
                                    wk_sb[:, c * HKV * HD + h * HD:
                                          c * HKV * HD + (h + 1) * HD],
                                    xl[:, bass.ts(c, CH)],
                                    start=(c == 0), stop=(c == DC - 1))
                ins.then_inc(pe.h, 1)
                kmm = pe.inc()
                if P1_NO_ROPE:
                    ko = ks[:, bass.ts(h, CH)]
                    VE.wait_ge(pe.h, kmm)
                    VE.tensor_copy(ko, kp[:, :]).then_inc(ve.h, 1)
                    kc_ev[h] = ve.inc()
                    rope_ev[h] = kc_ev[h]
                    rot_mm[h] = pe.n
                    continue
                VE.wait_ge(pe.h, kmm)
                if h >= 1:
                    VE.wait_ge(pe.h, rot_mm[h - 1])
                elif t > 0:
                    VE.wait_ge(pe.h, last_rot_mm)
                VE.tensor_copy(kc_sb[:, :], kp[:, :]).then_inc(ve.h, 1)
                kc_ev[h] = ve.inc()
                TE.wait_ge(ve.h, kc_ev[h])
                if h >= 1:
                    TE.wait_ge(ve.h, rope_ev[h - 1])
                TE.matmul(kr_ps[:, :], rsw_sb[:, :], kc_sb[:, :],
                          start=True, stop=True).then_inc(pe.h, 1)
                rot_mm[h] = pe.inc()
                ko = ks[:, bass.ts(h, CH)]
                VE.wait_ge(pe.h, rot_mm[h])
                VE.wait_ge(sr.h, rope_ld)
                VE.tensor_mul(tmpf[:, 0:CH], kc_sb[:, :], rope_cos)
                VE.tensor_mul(tmpf[:, CH:2 * CH], kr_ps[:, :], rope_sin)
                VE.tensor_add(ko, tmpf[:, 0:CH], tmpf[:, CH:2 * CH]
                              ).then_inc(ve.h, 1)
                rope_ev[h] = ve.inc()
            last_rot_mm = rot_mm[HKV - 1]
            rope_done[t] = ve.n
            kproj_done[t] = pe.n

            if P1_NO_KST:
                kdma_done[t] = s_kst[t % 2].n
                continue
            SY.wait_ge(ve.h, rope_done[t])
            SY.dma_start(out=T["kvk_d"].ap()[:, tc].rearrange(
                             "(h p) w -> p h w", p=128),
                         in_=ks.rearrange("p (h w) -> p h w", h=HKV)
                         ).then_inc(s_kst[t % 2].h, 16)
            kdma_done[t] = s_kst[t % 2].inc(16)


def xTap_slice(T, tc):
    return T["xT"].ap()[:, tc].rearrange("(c p) w -> p c w", p=128)


def phase2(nc, r, T, env):
    TE, VE, SE, GP, SY = nc.tensor, nc.vector, nc.scalar, nc.gpsimd, nc.sync
    dg, dsy = env["dg"], env["dsy"]
    pe, ve = env["pe"], env["ve"]
    NT = env["NT"]
    KV = HKV * HD
    s_wv = dg[0]
    s_xln = [dsy[0], dsy[1]]
    s_vst = [dsy[2], dsy[3]]

    with (
        nc.sbuf_tensor(f"p2wv{r}", [128, DC * KV], BF16) as wv_sb,
        nc.sbuf_tensor(f"p2xln{r}", [128, 2 * DC * CH], BF16) as xlnb,
        nc.sbuf_tensor(f"p2vst{r}", [128, 2 * KV], BF16) as vst,
        nc.psum_tensor(f"p2v0{r}", [128, KV], F32) as v0,
        nc.psum_tensor(f"p2v1{r}", [128, KV], F32) as v1,
    ):
        vps = [v0, v1]
        GP.dma_start(out=wv_sb[:, :].rearrange("p (c o) -> p c o", c=DC),
                     in_=T["wv"].ap().rearrange("(c p) o -> p c o", p=128)
                     ).then_inc(s_wv.h, 16)
        TE.wait_ge(s_wv.h, s_wv.inc(16))

        xln_ld = [None] * NT
        ev_done = {}
        vst_dma = {}
        for t in range(NT):
            tc = bass.ts(t, CH)
            xl = xlnb[:, bass.ts(t % 2, DC * CH)]
            if t > 1:
                SY.wait_ge(pe.h, mm_done[t - 2])
            SY.dma_start(out=xl.rearrange("p (c w) -> p c w", c=DC),
                         in_=T["xln_d"].ap()[:, tc].rearrange(
                             "(c p) w -> p c w", p=128)
                         ).then_inc(s_xln[t % 2].h, 16)
            xln_ld[t] = s_xln[t % 2].inc(16)
            if t == 0:
                mm_done = [None] * NT
            TE.wait_ge(s_xln[t % 2].h, xln_ld[t])
            for s in range(4):
                vp = vps[s % 2]
                key = (t, s)
                prev = (t, s - 2) if s >= 2 else (t - 1, s + 2)
                if prev in ev_done:
                    TE.wait_ge(ve.h, ev_done[prev])
                for c in range(DC):
                    lhsT = xl[:, c * CH + s * 128: c * CH + (s + 1) * 128]
                    for half in range(2):
                        ins = TE.matmul(vp[:, bass.ts(half, CH)],
                                        lhsT,
                                        wv_sb[:, c * KV + half * CH:
                                              c * KV + (half + 1) * CH],
                                        start=(c == 0), stop=(c == DC - 1))
                ins.then_inc(pe.h, 1)
                mmv = pe.inc()
                vs = vst[:, bass.ts(s % 2, KV)]
                pk = (t, s - 2) if s >= 2 else (t - 1, s + 2)
                if pk in vst_dma:
                    VE.wait_ge(s_vst[s % 2].h, vst_dma[pk])
                VE.wait_ge(pe.h, mmv)
                VE.tensor_copy(vs, vp[:, :]).then_inc(ve.h, 1)
                ev_done[key] = ve.inc()
                SY.wait_ge(ve.h, ev_done[key])
                SY.dma_start(out=T["kvv_d"].ap()[t * CH + s * 128:
                                                 t * CH + (s + 1) * 128, :],
                             in_=vs).then_inc(s_vst[s % 2].h, 16)
                vst_dma[key] = s_vst[s % 2].inc(16)
            mm_done[t] = pe.n


def phase3(nc, r, T, env):
    TE, VE, SE, GP, SY = nc.tensor, nc.vector, nc.scalar, nc.gpsimd, nc.sync
    dg, dsy = env["dg"], env["dsy"]
    pe, ve = env["pe"], env["ve"]
    NT, lo = env["NT"], env["lo"]
    s_xo = dsy[0]
    s_wq = [dg[0], dg[1]]
    s_qst = [dsy[1], dsy[2]]

    with (
        nc.sbuf_tensor(f"p3xln{r}", [128, DC * 1024], BF16) as xo,
        nc.sbuf_tensor(f"p3wq{r}", [128, 2 * DC * HD], BF16) as wqb,
        nc.sbuf_tensor(f"p3rope{r}", [128, 2 * 1024], F32) as ropeq,
        nc.sbuf_tensor(f"p3tmp{r}", [128, 2 * 1024], F32) as tmp,
        nc.sbuf_tensor(f"p3qc{r}", [128, 1024], BF16) as qc_sb,
        nc.sbuf_tensor(f"p3qst{r}", [128, 2 * 1024], BF16) as qst,
        nc.psum_tensor(f"p3q0{r}", [128, 1024], F32) as q0,
        nc.psum_tensor(f"p3q1{r}", [128, 1024], F32) as q1,
        nc.psum_tensor(f"p3qr{r}", [128, 1024], F32) as qr_ps,
    ):
        qps = [q0, q1]
        rsw_sb = env["rsw_sb"]
        for i, t in enumerate((lo, NT - 1)):
            SY.dma_start(
                out=xo[:, :].rearrange("p (c u w) -> p c u w", c=DC, u=2
                                       )[:, :, i, :],
                in_=T["xln_d"].ap()[:, bass.ts(t, CH)].rearrange(
                    "(c p) w -> p c w", p=128)).then_inc(s_xo.h, 16)
            s_xo.inc(16)
        SY.dma_start(out=ropeq[:, 0:1024], in_=T["cosq"].ap()).then_inc(s_xo.h, 16)
        SY.dma_start(out=ropeq[:, 1024:2048], in_=T["sinq"].ap()).then_inc(s_xo.h, 16)
        xo_ld = s_xo.inc(32)
        cosb, sinb = ropeq[:, 0:1024], ropeq[:, 1024:2048]

        TE.wait_ge(s_xo.h, xo_ld)
        VE.wait_ge(s_xo.h, xo_ld)
        rope_ev = {}
        qdma_ev = {}
        qc_ev = {}
        rot_mm = {}
        for oc in range(HQ):
            wqt = wqb[:, bass.ts(oc % 2, DC * HD)]
            if oc >= 2:
                GP.wait_ge(pe.h, qmm_ev[oc - 2])
            GP.dma_start(out=wqt.rearrange("p (c o) -> p c o", c=DC),
                         in_=T["wq"].ap()[:, bass.ts(oc, HD)].rearrange(
                             "(c p) o -> p c o", p=128)
                         ).then_inc(s_wq[oc % 2].h, 16)
            wq_ld = s_wq[oc % 2].inc(16)
            qp = qps[oc % 2]
            if oc == 0:
                qmm_ev = [None] * HQ
            TE.wait_ge(s_wq[oc % 2].h, wq_ld)
            if oc >= 2:
                TE.wait_ge(ve.h, qc_ev[oc - 2])
            for c in range(DC):
                for half in range(2):
                    ins = TE.matmul(qp[:, bass.ts(half, CH)],
                                    wqt[:, bass.ts(c, HD)],
                                    xo[:, c * 1024 + half * CH:
                                       c * 1024 + (half + 1) * CH],
                                    start=(c == 0), stop=(c == DC - 1))
            ins.then_inc(pe.h, 1)
            qmm_ev[oc] = pe.inc()
            VE.wait_ge(pe.h, qmm_ev[oc])
            if oc >= 1:
                VE.wait_ge(pe.h, rot_mm[oc - 1])
            VE.tensor_copy(qc_sb[:, :], qp[:, :]).then_inc(ve.h, 1)
            qc_ev[oc] = ve.inc()
            TE.wait_ge(ve.h, qc_ev[oc])
            if oc >= 1:
                TE.wait_ge(ve.h, rope_ev[oc - 1])
            for half in range(2):
                ins = TE.matmul(qr_ps[:, bass.ts(half, CH)], rsw_sb[:, :],
                                qc_sb[:, bass.ts(half, CH)],
                                start=True, stop=True)
            ins.then_inc(pe.h, 1)
            rot_mm[oc] = pe.inc()
            qo = qst[:, bass.ts(oc % 2, 1024)]
            VE.wait_ge(pe.h, rot_mm[oc])
            if oc >= 2:
                VE.wait_ge(s_qst[oc % 2].h, qdma_ev[oc - 2])
            VE.tensor_mul(tmp[:, 0:1024], qc_sb[:, :], cosb)
            VE.tensor_mul(tmp[:, 1024:2048], qr_ps[:, :], sinb)
            VE.tensor_add(qo, tmp[:, 0:1024], tmp[:, 1024:2048]
                          ).then_inc(ve.h, 1)
            rope_ev[oc] = ve.inc()
            SY.wait_ge(ve.h, rope_ev[oc])
            SY.dma_start(out=T["q_d"].ap()[bass.ts(oc, HD), :],
                         in_=qo).then_inc(s_qst[oc % 2].h, 16)
            qdma_ev[oc] = s_qst[oc % 2].inc(16)


def spans_for(c, r):
    pieces = []
    tris = []
    if c < 4 * r:
        pieces.append((0, 512))
    elif c < 4 * r + 4:
        j0 = 128 * (c - 4 * r)
        pieces.append((j0, 512))
        tris.append(j0)
    ch = c - 4 * (15 - r)
    if ch < 0:
        pieces.append((512, 1024))
    elif ch < 4:
        j0 = 128 * ch
        pieces.append((512 + j0, 1024))
        tris.append(512 + j0)
    merged = []
    for p in pieces:
        if merged and merged[-1][1] == p[0]:
            merged[-1] = (merged[-1][0], p[1])
        else:
            merged.append(list(p) and p)
    return merged, tris


def phase4(nc, r, T, env):
    TE, VE, SE, GP, SY = nc.tensor, nc.vector, nc.scalar, nc.gpsimd, nc.sync
    dg, dsy = env["dg"], env["dsy"]
    pe, ve, se = env["pe"], env["ve"], env["se"]
    NT, P2 = env["NT"], env["P2"]
    ones1b, ones1f, ebias, tri_sb = (env["ones1b"], env["ones1f"],
                                     env["ebias"], env["tri_sb"])
    NKC = P2 // 128
    s_kv = [dsy[0], dsy[1]]
    s_q = [dsy[2], dsy[3]]
    s_ost = [dsy[4], dsy[5]]
    s_ldbg = dsy[6]

    with (
        nc.sbuf_tensor(f"p4k{r}", [128, 2 * S], BF16) as kbuf,
        nc.sbuf_tensor(f"p4v{r}", [128, 2 * S], BF16) as vbuf,
        nc.sbuf_tensor(f"p4q{r}", [128, 2 * 1024], BF16) as qbuf,
        nc.sbuf_tensor(f"p4pt{r}", [128, 2 * 1024], BF16) as ptb,
        nc.sbuf_tensor(f"p4lrow{r}", [1, 3 * 1024], F32) as lrow,
        nc.sbuf_tensor(f"p4linv{r}", [128, 1024], F32) as linvb,
        nc.sbuf_tensor(f"p4ost{r}", [128, 2 * 1024], BF16) as ost,
        nc.psum_tensor(f"p4s0{r}", [128, 1024], F32) as s0,
        nc.psum_tensor(f"p4s1{r}", [128, 1024], F32) as s1,
        nc.psum_tensor(f"p4o{r}", [128, 1024], F32) as o_ps,
        nc.psum_tensor(f"p4l{r}", [1, 1024], F32) as l_ps,
    ):
        sps = [s0, s1]
        kv_ld = {}
        kv_freed = {}
        q_ld = {}
        exp_ev = {}
        mask_ev = {}
        pvl_ev = {}
        smm_ev = {}
        evac_ev = {}
        odma_ev = {}

        def load_kv(h, slot):
            SY.dma_start(out=kbuf[:, slot * S: slot * S + P2],
                         in_=T["kvk_d"].ap()[bass.ts(h, HD), 0:P2]
                         ).then_inc(s_kv[slot].h, 16)
            SY.dma_start(out=vbuf[:, slot * S: slot * S + P2].rearrange(
                             "p (c o) -> p c o", o=HD),
                         in_=T["kvv_d"].ap()[0:P2, bass.ts(h, HD)].rearrange(
                             "(c p) o -> p c o", p=128)
                         ).then_inc(s_kv[slot].h, 16)
            kv_ld[h] = s_kv[slot].inc(32)

        load_kv(0, 0)
        for kvh in range(HKV):
            slot = kvh % 2
            if kvh + 1 < HKV:
                if kvh >= 1:
                    SY.wait_ge(pe.h, kv_freed[kvh - 1])
                load_kv(kvh + 1, (kvh + 1) % 2)
            for qh in range(4):
                g = 4 * kvh + qh
                qslot = g % 2
                if g >= 2:
                    SY.wait_ge(pe.h, pvl_last[g - 2])
                SY.dma_start(out=qbuf[:, bass.ts(qslot, 1024)],
                             in_=T["q_d"].ap()[bass.ts(g, HD), :]
                             ).then_inc(s_q[qslot].h, 16)
                q_ld[g] = s_q[qslot].inc(16)
                if g == 0:
                    pvl_last = {}
                qt = qbuf[:, bass.ts(qslot, 1024)]
                TE.wait_ge(s_kv[slot].h, kv_ld[kvh])
                TE.wait_ge(s_q[qslot].h, q_ld[g])
                if g >= 1:
                    TE.wait_ge(ve.h, evac_ev[g - 1])
                c_last_lo = 4 * r + 3

                def emit_pvl(c):
                    spans, tris = spans_for(c, r)
                    pt = ptb[:, bass.ts(c % 2, 1024)]
                    vt = vbuf[:, slot * S + c * 128: slot * S + (c + 1) * 128]
                    TE.wait_ge(se.h, exp_ev[(g, c)])
                    if (g, c) in mask_ev:
                        TE.wait_ge(ve.h, mask_ev[(g, c)])
                    first = (c == 0)
                    for (a, b) in spans:
                        x = a
                        while x < b:
                            e = min(b, (x // 512 + 1) * 512)
                            last = (c == (c_last_lo if x < 512 else NKC - 1))
                            TE.matmul(o_ps[:, x:e], vt, pt[:, x:e],
                                      start=first, stop=last)
                            ins = TE.matmul(l_ps[:, x:e], ones1b[:, :],
                                            pt[:, x:e], start=first, stop=last)
                            x = e
                    ins.then_inc(pe.h, 1)
                    pvl_ev[(g, c)] = pe.inc()

                for c in range(NKC):
                    spans, tris = spans_for(c, r)
                    sp = sps[c % 2]
                    kt = kbuf[:, slot * S + c * 128: slot * S + (c + 1) * 128]
                    key = (g, c)
                    if c >= 2:
                        TE.wait_ge(se.h, exp_ev[(g, c - 2)])
                    elif g >= 1 and c < 2:
                        TE.wait_ge(se.h, exp_last[g - 1])
                    for (a, b) in spans:
                        x = a
                        while x < b:
                            e = min(b, (x // 512 + 1) * 512)
                            ins = TE.matmul(sp[:, x:e], kt, qt[:, x:e],
                                            start=True, stop=True)
                            x = e
                    ins.then_inc(pe.h, 1)
                    smm_ev[key] = pe.inc()
                    pt = ptb[:, bass.ts(c % 2, 1024)]
                    SE.wait_ge(pe.h, smm_ev[key])
                    if c >= 2:
                        SE.wait_ge(pe.h, pvl_ev[(g, c - 2)])
                    for i, (a, b) in enumerate(spans):
                        ins = SE.activation(pt[:, a:b], sp[:, a:b], AF.Exp,
                                            bias=ebias[:, :])
                        if i == len(spans) - 1:
                            ins.then_inc(se.h, 1)
                    exp_ev[key] = se.inc()
                    if tris:
                        VE.wait_ge(se.h, exp_ev[key])
                        for o in tris:
                            VE.tensor_mul(pt[:, o:o + 128], pt[:, o:o + 128],
                                          tri_sb[:, :])
                        VE.engine_nop().then_inc(ve.h, 1)
                        mask_ev[key] = ve.inc()
                    if c >= 1:
                        emit_pvl(c - 1)
                emit_pvl(NKC - 1)
                exp_last = exp_last if g > 0 else {}
                exp_last[g] = exp_ev[(g, NKC - 1)]
                pvl_last[g] = pvl_ev[(g, NKC - 1)]
                if qh == 3:
                    kv_freed[kvh] = pvl_last[g]

                lr = lrow[:, 0:1024]
                li = lrow[:, 1024:2048]
                VE.wait_ge(pe.h, pvl_last[g])
                VE.wait_ge(s_ldbg.h, s_ldbg.n)
                VE.tensor_copy(lr, l_ps[:, :])
                VE.reciprocal(li, lr).then_inc(ve.h, 1)
                vv = ve.inc()
                TE.wait_ge(ve.h, vv)
                bcp = sps[0]
                TE.matmul(bcp[:, 0:512], ones1f[:, :], li[:, 0:512],
                          start=True, stop=True)
                TE.matmul(bcp[:, 512:1024], ones1f[:, :], li[:, 512:1024],
                          start=True, stop=True).then_inc(pe.h, 1)
                bc = pe.inc()
                VE.wait_ge(pe.h, bc)
                VE.tensor_copy(linvb[:, :], bcp[:, :])
                oo = ost[:, bass.ts(g % 2, 1024)]
                if g >= 2:
                    VE.wait_ge(s_ost[g % 2].h, odma_ev[g - 2])
                VE.tensor_mul(oo, o_ps[:, :], linvb[:, :]).then_inc(ve.h, 1)
                evac_ev[g] = ve.inc()
                SY.wait_ge(ve.h, evac_ev[g])
                SY.dma_start(out=T["o_d"].ap()[bass.ts(g, HD), :],
                             in_=oo).then_inc(s_ost[g % 2].h, 16)
                odma_ev[g] = s_ost[g % 2].inc(16)
                SY.dma_start(out=T["l_d"].ap()[g:g + 1, :], in_=li
                             ).then_inc(s_ldbg.h, 16)
                s_ldbg.inc(16)


def phase56(nc, r, T, env):
    TE, VE, SE, GP, SY = nc.tensor, nc.vector, nc.scalar, nc.gpsimd, nc.sync
    dg, dsy = env["dg"], env["dsy"]
    pe, ve, se = env["pe"], env["ve"], env["se"]
    lo0, hi0 = env["lo0"], env["hi0"]
    ones1b, ones1f, epst = env["ones1b"], env["ones1f"], env["epst"]
    onescf = env["onescf"]
    s_o = dsy[0]
    s_wo = [dg[0], dg[1]]
    s_x = [dsy[1], dsy[2]]
    s_xast = [dsy[3], dsy[4]]
    s_x2 = [dsy[5], dsy[6]]
    s_x2st = [dsy[7], dsy[8]]

    with (
        nc.sbuf_tensor(f"p5o{r}", [128, DC * 1024], BF16) as ob,
        nc.sbuf_tensor(f"p5wo{r}", [128, 2 * DC * HD], BF16) as wob,
        nc.sbuf_tensor(f"p5x{r}", [128, 2 * 1024], F32) as xb,
        nc.sbuf_tensor(f"p5xa{r}", [128, 2 * 1024], F32) as xab,
        nc.sbuf_tensor(f"p5sq{r}", [128, 2 * 1024], BF16) as sqb,
        nc.sbuf_tensor(f"p5rows{r}", [1, 6 * 1024], F32) as rows,
        nc.sbuf_tensor(f"p5ab{r}", [128, 2 * 1024], F32) as abb,
        nc.sbuf_tensor(f"p5xl2{r}", [128, 2 * 1024], BF16) as xl2b,
        nc.psum_tensor(f"p5w0{r}", [128, 1024], F32) as w0,
        nc.psum_tensor(f"p5w1{r}", [128, 1024], F32) as w1,
        nc.psum_tensor(f"p5sum{r}", [1, 1024], F32) as sum_ps,
        nc.psum_tensor(f"p5sq2{r}", [1, 1024], F32) as sq_ps,
    ):
        wps = [w0, w1]
        SY.dma_start(out=ob[:, :].rearrange("p (a w) -> p a w", a=DC),
                     in_=T["o_d"].ap().rearrange("(a p) w -> p a w", p=128)
                     ).then_inc(s_o.h, 16)
        o_ld = s_o.inc(16)
        TE.wait_ge(s_o.h, o_ld)

        xld_ev = {}
        wo_ld = {}
        mm_ev = {}
        ev_ev = {}
        sq_ev = {}
        st_ev = {}
        xa_dma = {}

        def load_x(dch, slot):
            x = xb[:, bass.ts(slot, 1024)]
            SY.dma_start(out=x[:, 0:512],
                         in_=T["xT"].ap()[bass.ts(dch, 128),
                                          lo0:lo0 + 512]).then_inc(s_x[slot].h, 16)
            SY.dma_start(out=x[:, 512:1024],
                         in_=T["xT"].ap()[bass.ts(dch, 128),
                                          hi0:hi0 + 512]).then_inc(s_x[slot].h, 16)
            xld_ev[dch] = s_x[slot].inc(32)

        def load_wo(dch, slot):
            GP.dma_start(out=wob[:, bass.ts(slot, DC * HD)].rearrange(
                             "p (c o) -> p c o", c=DC),
                         in_=T["wo"].ap()[:, bass.ts(dch, HD)].rearrange(
                             "(c p) o -> p c o", p=128)).then_inc(s_wo[slot].h, 16)
            wo_ld[dch] = s_wo[slot].inc(16)

        load_x(0, 0)
        load_wo(0, 0)
        for dch in range(DC):
            slot = dch % 2
            if dch + 1 < DC:
                if dch >= 1:
                    GP.wait_ge(pe.h, mm_ev[dch - 1])
                    SY.wait_ge(ve.h, ev_ev[dch - 1])
                load_x(dch + 1, (dch + 1) % 2)
                load_wo(dch + 1, (dch + 1) % 2)
            wp = wps[slot]
            TE.wait_ge(s_wo[slot].h, wo_ld[dch])
            if dch >= 2:
                TE.wait_ge(ve.h, ev_ev[dch - 2])
            for a in range(DC):
                for half in range(2):
                    ins = TE.matmul(wp[:, bass.ts(half, CH)],
                                    wob[:, slot * DC * HD + a * HD:
                                        slot * DC * HD + (a + 1) * HD],
                                    ob[:, a * 1024 + half * CH:
                                       a * 1024 + (half + 1) * CH],
                                    start=(a == 0), stop=(a == DC - 1))
            ins.then_inc(pe.h, 1)
            mm_ev[dch] = pe.inc()
            xa = xab[:, bass.ts(slot, 1024)]
            VE.wait_ge(pe.h, mm_ev[dch])
            VE.wait_ge(s_x[slot].h, xld_ev[dch])
            if dch >= 2:
                VE.wait_ge(s_xast[slot].h, xa_dma[dch - 2])
                VE.wait_ge(pe.h, st_ev[dch - 2])
            VE.tensor_add(xa, wp[:, :], xb[:, bass.ts(slot, 1024)]
                          ).then_inc(ve.h, 1)
            ev_ev[dch] = ve.inc()
            SY.wait_ge(ve.h, ev_ev[dch])
            SY.dma_start(out=T["xattn_d"].ap()[bass.ts(dch, 128), :],
                         in_=xa).then_inc(s_xast[slot].h, 16)
            xa_dma[dch] = s_xast[slot].inc(16)
            sq = sqb[:, bass.ts(slot, 1024)]
            SE.wait_ge(ve.h, ev_ev[dch])
            if dch >= 2:
                SE.wait_ge(pe.h, st_ev[dch - 2])
            SE.square(sq, xa).then_inc(se.h, 1)
            sq_ev[dch] = se.inc()
            TE.wait_ge(se.h, sq_ev[dch])
            for half in range(2):
                hs = bass.ts(half, CH)
                TE.matmul(sum_ps[:, hs], onescf[:, :], xa[:, hs],
                          start=(dch == 0), stop=(dch == DC - 1))
                ins = TE.matmul(sq_ps[:, hs], ones1b[:, :], sq[:, hs],
                                start=(dch == 0), stop=(dch == DC - 1))
            ins.then_inc(pe.h, 1)
            st_ev[dch] = pe.inc()

        mean = rows[:, 0:1024]
        ex2 = rows[:, 1024:2048]
        var = rows[:, 2048:3072]
        sqv = rows[:, 3072:4096]
        a_row = rows[:, 4096:5120]
        b_row = rows[:, 5120:6144]
        VE.wait_ge(pe.h, st_ev[DC - 1])
        VE.tensor_scalar_mul(mean, sum_ps[:, :], 1.0 / D)
        VE.tensor_scalar_mul(ex2, sq_ps[:, :], 1.0 / D)
        VE.tensor_mul(var, mean, mean)
        VE.tensor_sub(var, ex2, var).then_inc(ve.h, 1)
        vv = ve.inc()
        SE.wait_ge(ve.h, vv)
        SE.activation(sqv, var, AF.Sqrt, bias=epst[:, :]).then_inc(se.h, 1)
        sv = se.inc()
        VE.wait_ge(se.h, sv)
        VE.reciprocal(a_row, sqv)
        VE.tensor_mul(b_row, mean, a_row).then_inc(ve.h, 1)
        rows_done = ve.inc()
        TE.wait_ge(ve.h, rows_done)
        for half in range(2):
            hs = bass.ts(half, CH)
            TE.matmul(w0[:, hs], ones1f[:, :], a_row[:, hs],
                      start=True, stop=True)
            ins = TE.matmul(w1[:, hs], ones1f[:, :], b_row[:, hs],
                            start=True, stop=True)
        ins.then_inc(pe.h, 1)
        bc = pe.inc()
        VE.wait_ge(pe.h, bc)
        VE.tensor_copy(abb[:, 0:1024], w0[:, :])
        VE.tensor_copy(abb[:, 1024:2048], w1[:, :]).then_inc(ve.h, 1)
        ab_ok = ve.inc()

        x2ld = {}
        x2ev = {}
        SY.wait_ge(pe.h, st_ev[DC - 1])
        for dch in range(DC):
            slot = dch % 2
            xa = xab[:, bass.ts(slot, 1024)]
            if dch >= 2:
                SY.wait_ge(ve.h, x2ev[dch - 2])
            SY.dma_start(out=xa, in_=T["xattn_d"].ap()[bass.ts(dch, 128), :]
                         ).then_inc(s_x2[slot].h, 16)
            x2ld[dch] = s_x2[slot].inc(16)
            xl2 = xl2b[:, bass.ts(slot, 1024)]
            VE.wait_ge(s_x2[slot].h, x2ld[dch])
            VE.wait_ge(ve.h, ab_ok)
            if dch >= 2:
                VE.wait_ge(s_x2st[slot].h, x2st[dch - 2])
            VE.tensor_mul(xa, xa, abb[:, 0:1024])
            VE.tensor_sub(xl2, xa, abb[:, 1024:2048]).then_inc(ve.h, 1)
            x2ev[dch] = ve.inc()
            if dch == 0:
                x2st = {}
            SY.wait_ge(ve.h, x2ev[dch])
            SY.dma_start(out=T["xln2_d"].ap()[bass.ts(dch, 128), :],
                         in_=xl2).then_inc(s_x2st[slot].h, 16)
            x2st[dch] = s_x2st[slot].inc(16)


def phase7(nc, r, T, env):
    TE, VE, SE, GP, SY = nc.tensor, nc.vector, nc.scalar, nc.gpsimd, nc.sync
    dg, dsy = env["dg"], env["dsy"]
    pe, ve, se = env["pe"], env["ve"], env["se"]
    s_lora, s_xl = dg[0], dsy[0]
    s_gu = [dg[1], dg[2]]
    s_dn = [dg[3], dg[4]]
    s_xa = [dsy[1], dsy[2]]
    s_out = [dsy[3], dsy[4]]

    with (
        nc.sbuf_tensor(f"p7hid{r}", [128, HC * CH], BF16) as hid,
        nc.sbuf_tensor(f"p7shr{r}", [128, 2 * HC * HD], BF16) as shr,
        nc.sbuf_tensor(f"p7gu{r}", [128, 2 * 2 * DC * HD], BF16) as gub,
        nc.sbuf_tensor(f"p7ab{r}", [128, DC * RK * 2], BF16) as lab,
        nc.sbuf_tensor(f"p7da{r}", [128, HC * RK], BF16) as dab,
        nc.sbuf_tensor(f"p7bgu{r}", [RK, 2 * 2 * HD], BF16) as bgu,
        nc.sbuf_tensor(f"p7db{r}", [RK, D], BF16) as dbb,
        nc.sbuf_tensor(f"p7z{r}", [RK, 3 * CH], BF16) as zb,
        nc.sbuf_tensor(f"p7sil{r}", [128, 2 * CH], BF16) as sil,
        nc.sbuf_tensor(f"p7xa{r}", [128, 2 * CH], F32) as xab,
        nc.sbuf_tensor(f"p7out{r}", [128, 2 * CH], F32) as outb,
        nc.psum_tensor(f"p7g0{r}", [128, CH], F32) as g0,
        nc.psum_tensor(f"p7g1{r}", [128, CH], F32) as g1,
        nc.psum_tensor(f"p7u0{r}", [128, CH], F32) as u0,
        nc.psum_tensor(f"p7u1{r}", [128, CH], F32) as u1,
        nc.psum_tensor(f"p7zps{r}", [RK, 2 * CH], F32) as z_ps,
        nc.psum_tensor(f"p7zd{r}", [RK, CH], F32) as zd_ps,
        nc.psum_tensor(f"p7d0{r}", [128, CH], F32) as d0,
    ):
        xl2 = shr[:, 0:DC * CH]
        dwb = shr
        gps, ups = [g0, g1], [u0, u1]
        GP.dma_start(out=lab[:, 0:DC * RK].rearrange("p (c k) -> p c k", c=DC),
                     in_=T["gaT"].ap().rearrange("(c p) k -> p c k", p=128)
                     ).then_inc(s_lora.h, 16)
        GP.dma_start(out=lab[:, DC * RK:].rearrange("p (c k) -> p c k", c=DC),
                     in_=T["uaT"].ap().rearrange("(c p) k -> p c k", p=128)
                     ).then_inc(s_lora.h, 16)
        GP.dma_start(out=dab[:, :].rearrange("p (c k) -> p c k", c=HC),
                     in_=T["daT"].ap().rearrange("(c p) k -> p c k", p=128)
                     ).then_inc(s_lora.h, 16)
        GP.dma_start(out=dbb[:, :], in_=T["dbT"].ap()).then_inc(s_lora.h, 16)
        lora_ld = s_lora.inc(64)
        TE.wait_ge(s_lora.h, lora_ld)

        prev_tile_pe = None
        for tile in range(2):
            tc = bass.ts(tile, CH)
            if prev_tile_pe is not None:
                SY.wait_ge(pe.h, prev_tile_pe)
            SY.dma_start(out=xl2.rearrange("p (c w) -> p c w", c=DC),
                         in_=T["xln2_d"].ap()[:, tc].rearrange(
                             "(c p) w -> p c w", p=128)).then_inc(s_xl.h, 16)
            xl_ld = s_xl.inc(16)
            TE.wait_ge(s_xl.h, xl_ld)
            if tile == 1:
                TE.wait_ge(ve.h, z_ev)
                TE.wait_ge(ve.h, prev_out_ve)
                VE.wait_ge(pe.h, prev_tile_pe)
            for c in range(DC):
                xc = xl2[:, bass.ts(c, CH)]
                TE.matmul(z_ps[:, 0:CH], lab[:, c * RK:(c + 1) * RK], xc,
                          start=(c == 0), stop=(c == DC - 1))
                ins = TE.matmul(z_ps[:, CH:2 * CH],
                                lab[:, DC * RK + c * RK:
                                    DC * RK + (c + 1) * RK],
                                xc, start=(c == 0), stop=(c == DC - 1))
            ins.then_inc(pe.h, 1)
            zmm = pe.inc()
            VE.wait_ge(pe.h, zmm)
            VE.tensor_copy(zb[:, 0:2 * CH], z_ps[:, :]).then_inc(ve.h, 1)
            z_ev = ve.inc()

            gu_ld = {}
            gumm = {}
            hid_ev = {}
            for h in range(HC):
                slot = h % 2
                if h >= 2:
                    GP.wait_ge(pe.h, gumm[h - 2])
                GP.dma_start(
                    out=gub[:, slot * 2 * DC * HD: slot * 2 * DC * HD + DC * HD
                            ].rearrange("p (c o) -> p c o", c=DC),
                    in_=T["gwT"].ap()[:, bass.ts(h, HD)].rearrange(
                        "(c p) o -> p c o", p=128)).then_inc(s_gu[slot].h, 16)
                GP.dma_start(
                    out=gub[:, slot * 2 * DC * HD + DC * HD:
                            (slot + 1) * 2 * DC * HD
                            ].rearrange("p (c o) -> p c o", c=DC),
                    in_=T["uwT"].ap()[:, bass.ts(h, HD)].rearrange(
                        "(c p) o -> p c o", p=128)).then_inc(s_gu[slot].h, 16)
                GP.dma_start(out=bgu[:, bass.ts(slot, 2 * HD)],
                             in_=T["gubT"].ap()[:, bass.ts(h, 2 * HD)]
                             ).then_inc(s_gu[slot].h, 16)
                gu_ld[h] = s_gu[slot].inc(48)
                gp_, up_ = gps[slot], ups[slot]
                TE.wait_ge(s_gu[slot].h, gu_ld[h])
                TE.wait_ge(ve.h, z_ev)
                if h >= 2:
                    TE.wait_ge(ve.h, hid_ev[h - 2])
                gbase = slot * 2 * DC * HD
                for c in range(DC):
                    xc = xl2[:, bass.ts(c, CH)]
                    TE.matmul(gp_[:, :], gub[:, gbase + c * HD:
                                             gbase + (c + 1) * HD],
                              xc, start=(c == 0), stop=False)
                    TE.matmul(up_[:, :], gub[:, gbase + DC * HD + c * HD:
                                             gbase + DC * HD + (c + 1) * HD],
                              xc, start=(c == 0), stop=False)
                TE.matmul(gp_[:, :], bgu[:, slot * 2 * HD: slot * 2 * HD + HD],
                          zb[:, 0:CH], start=False, stop=True)
                ins = TE.matmul(up_[:, :], bgu[:, slot * 2 * HD + HD:
                                               (slot + 1) * 2 * HD],
                                zb[:, CH:2 * CH], start=False, stop=True)
                ins.then_inc(pe.h, 1)
                gumm[h] = pe.inc()
                sl = sil[:, bass.ts(slot, CH)]
                SE.wait_ge(pe.h, gumm[h])
                if h >= 2:
                    SE.wait_ge(ve.h, hid_ev[h - 2])
                SE.activation(sl, gp_[:, :], AF.Silu).then_inc(se.h, 1)
                sil_ev = se.inc()
                hc_ = hid[:, bass.ts(h, CH)]
                VE.wait_ge(se.h, sil_ev)
                VE.tensor_mul(hc_, sl, up_[:, :]).then_inc(ve.h, 1)
                hid_ev[h] = ve.inc()
                TE.wait_ge(ve.h, hid_ev[h])
                if tile == 1 and h == 0:
                    TE.wait_ge(ve.h, zd_ev)
                ins = TE.matmul(zd_ps[:, :], dab[:, h * RK:(h + 1) * RK], hc_,
                                start=(h == 0), stop=(h == HC - 1))
                if h == HC - 1:
                    ins.then_inc(pe.h, 1)
                    zd_mm = pe.inc()
            VE.wait_ge(pe.h, zd_mm)
            VE.tensor_copy(zb[:, 2 * CH:3 * CH], zd_ps[:, :]).then_inc(ve.h, 1)
            zd_ev = ve.inc()

            dn_ld = {}
            dnmm = {}
            out_ev = {}
            xa_ld = {}
            ost_ev = {}
            GP.wait_ge(pe.h, gumm[HC - 1])
            for dch in range(DC):
                slot = dch % 2
                if dch >= 2:
                    GP.wait_ge(pe.h, dnmm[dch - 2])
                GP.dma_start(out=dwb[:, bass.ts(slot, HC * HD)].rearrange(
                                 "p (c o) -> p c o", c=HC),
                             in_=T["dwT"].ap()[:, bass.ts(dch, HD)].rearrange(
                                 "(c p) o -> p c o", p=128)
                             ).then_inc(s_dn[slot].h, 16)
                dn_ld[dch] = s_dn[slot].inc(16)
                xa = xab[:, bass.ts(slot, CH)]
                if dch >= 2:
                    SY.wait_ge(ve.h, out_ev[dch - 2])
                SY.dma_start(out=xa,
                             in_=T["xattn_d"].ap()[bass.ts(dch, 128), tc]
                             ).then_inc(s_xa[slot].h, 16)
                xa_ld[dch] = s_xa[slot].inc(16)
                d_ = d0 if slot == 0 else g0
                TE.wait_ge(s_dn[slot].h, dn_ld[dch])
                TE.wait_ge(ve.h, zd_ev)
                if dch >= 2:
                    TE.wait_ge(ve.h, out_ev[dch - 2])
                dbase = slot * HC * HD
                for hcc in range(HC):
                    TE.matmul(d_[:, :], dwb[:, dbase + hcc * HD:
                                            dbase + (hcc + 1) * HD],
                              hid[:, bass.ts(hcc, CH)],
                              start=(hcc == 0), stop=False)
                ins = TE.matmul(d_[:, :], dbb[:, bass.ts(dch, HD)],
                                zb[:, 2 * CH:3 * CH], start=False, stop=True)
                ins.then_inc(pe.h, 1)
                dnmm[dch] = pe.inc()
                oo = outb[:, bass.ts(slot, CH)]
                VE.wait_ge(pe.h, dnmm[dch])
                VE.wait_ge(s_xa[slot].h, xa_ld[dch])
                if dch >= 2:
                    VE.wait_ge(s_out[slot].h, ost_ev[dch - 2])
                VE.tensor_add(oo, d_[:, :], xa).then_inc(ve.h, 1)
                out_ev[dch] = ve.inc()
                SY.wait_ge(ve.h, out_ev[dch])
                SY.dma_start(out=T["out_fm"].ap()[bass.ts(dch, 128), tc],
                             in_=oo).then_inc(s_out[slot].h, 16)
                ost_ev[dch] = s_out[slot].inc(16)
            prev_tile_pe = dnmm[DC - 1]
            prev_out_ve = out_ev[DC - 1]


def _rsw_mat():
    m = np.zeros((128, 128), dtype=np.float32)
    for d in range(64):
        m[d + 64, d] = -1.0
        m[d, d + 64] = 1.0
    return m.astype(ml_dtypes.bfloat16)


def _rope_tables():
    inv = 1.0 / (10000.0 ** (np.arange(0, HD, 2, dtype=np.float64) / HD))
    t = np.arange(S, dtype=np.float64)
    fr = np.outer(t, inv)
    emb = np.concatenate([fr, fr], axis=1)
    return (np.cos(emb).astype(np.float32).T.copy(),
            np.sin(emb).astype(np.float32).T.copy())


def kernel(L0_attn_hidden_states, L0_attn_Wq, L0_attn_Wk, L0_attn_Wv,
           L0_attn_Wo, L0_ffn_gate_w, L0_ffn_up_w, L0_ffn_down_w,
           L0_ffn_gate_a, L0_ffn_gate_b, L0_ffn_up_a, L0_ffn_up_b,
           L0_ffn_down_a, L0_ffn_down_b):
    x = np.asarray(L0_attn_hidden_states, dtype=np.float32).reshape(S, D)
    xT = np.ascontiguousarray(x.T)
    f32 = lambda a: np.ascontiguousarray(np.asarray(a, dtype=np.float32))
    wq, wk, wv, wo = f32(L0_attn_Wq), f32(L0_attn_Wk), f32(L0_attn_Wv), f32(L0_attn_Wo)
    gwT = np.ascontiguousarray(f32(L0_ffn_gate_w).T)
    uwT = np.ascontiguousarray(f32(L0_ffn_up_w).T)
    dwT = np.ascontiguousarray(f32(L0_ffn_down_w).T)
    gaT = np.ascontiguousarray(f32(L0_ffn_gate_a).T)
    uaT = np.ascontiguousarray(f32(L0_ffn_up_a).T)
    daT = np.ascontiguousarray(f32(L0_ffn_down_a).T)
    gbT = np.ascontiguousarray(f32(L0_ffn_gate_b).T)
    ubT = np.ascontiguousarray(f32(L0_ffn_up_b).T)
    dbT = np.ascontiguousarray(f32(L0_ffn_down_b).T)
    gub = np.stack([gbT.reshape(RK, HC, HD), ubT.reshape(RK, HC, HD)],
                   axis=2).reshape(RK, 2 * HID)
    gub = np.ascontiguousarray(gub)

    cosT, sinT = _rope_tables()
    scale = 1.0 / np.sqrt(np.float32(HD))
    tri = np.triu(np.ones((128, 128), dtype=np.float32))
    tri = tri.astype(ml_dtypes.bfloat16)
    rsw = _rsw_mat()

    nc = build_program(debug=False)

    shared = dict(xT=xT, wq=wq, wk=wk, wv=wv, wo=wo, gwT=gwT, uwT=uwT,
                  dwT=dwT, gaT=gaT, uaT=uaT, daT=daT, gubT=gub, dbT=dbT,
                  cosk=cosT, sink=sinT, tri=tri, rsw=rsw)
    in_maps = []
    for r in range(NCORES):
        lo, hi = rank_chunks(r)
        cols = np.r_[lo * CH:(lo + 1) * CH, hi * CH:(hi + 1) * CH]
        m = dict(shared)
        m["cosq"] = np.ascontiguousarray(cosT[:, cols] * scale)
        m["sinq"] = np.ascontiguousarray(sinT[:, cols] * scale)
        in_maps.append(m)

    res = run_bass_kernel_spmd(nc, in_maps, core_ids=list(range(NCORES)))

    out = np.empty((S, D), dtype=np.float32)
    for r in range(NCORES):
        lo, hi = rank_chunks(r)
        of = res.results[r]["out_fm"]
        out[lo * CH:(lo + 1) * CH] = of[:, 0:CH].T
        out[hi * CH:(hi + 1) * CH] = of[:, CH:1024].T
    return out.reshape(1, S, D)


# version 04
# speedup vs baseline: 1.3484x; 1.0551x over previous
import numpy as np
import ml_dtypes
import concourse.bass as bass
import concourse.mybir as mybir
from concourse.bass_utils import run_bass_kernel_spmd

F32 = mybir.dt.float32
BF16 = mybir.dt.bfloat16
AF = mybir.ActivationFunctionType
ALU = mybir.AluOpType

D = 4096
S = 8192
HKV = 8
HQ = 32
HD = 128
HID = 11008
RK = 16
CH = 512
NCH = S // CH
DC = D // 128
HC = HID // 128
EPS = 1e-5
EXP_SHIFT = -12.0
NCORES = 8
STOP_AFTER = 7
P1_TILES = None
P1_NO_K = False
P1_NO_XLN = False
P1_NO_ROPE = False
P1_NO_KST = False


class SemC:

    def __init__(self, h):
        self.h = h
        self.n = 0

    def inc(self, k=1):
        self.n += k
        return self.n


NDMA_G = 5
NDMA_S = 9


def rank_chunks(r):
    return r, 15 - r


def build_program(debug=False):
    nc = bass.Bass("TRN2", target_bir_lowering=False,
                   detect_race_conditions=False)

    t_in = {}

    def din(name, shape, dt=F32):
        t_in[name] = nc.dram_tensor(name, shape, dt, kind="ExternalInput")
        return t_in[name]

    xT = din("xT", [D, S])
    wq = din("wq", [D, D])
    wk = din("wk", [D, HKV * HD])
    wv = din("wv", [D, HKV * HD])
    wo = din("wo", [D, D])
    gwT = din("gwT", [D, HID])
    uwT = din("uwT", [D, HID])
    dwT = din("dwT", [HID, D])
    gaT = din("gaT", [D, RK])
    uaT = din("uaT", [D, RK])
    daT = din("daT", [HID, RK])
    gubT = din("gubT", [RK, 2 * HID])
    dbT = din("dbT", [RK, D])
    cosq = din("cosq", [HD, 1024])
    sinq = din("sinq", [HD, 1024])
    cosk = din("cosk", [HD, S])
    sink = din("sink", [HD, S])
    tri = din("tri", [128, 128], BF16)
    rsw = din("rsw", [128, 128], BF16)

    out_fm = nc.dram_tensor("out_fm", [D, 1024], F32, kind="ExternalOutput")

    dbg_kind = "ExternalOutput" if debug else "Internal"
    xln_d = nc.dram_tensor("xln_d", [D, S], BF16, kind=dbg_kind)
    kvk_d = nc.dram_tensor("kvk_d", [HKV * HD, S], BF16, kind=dbg_kind)
    kvv_d = nc.dram_tensor("kvv_d", [S, HKV * HD], BF16, kind=dbg_kind)
    q_d = nc.dram_tensor("q_d", [D, 1024], BF16, kind=dbg_kind)
    o_d = nc.dram_tensor("o_d", [D, 1024], BF16, kind=dbg_kind)
    l_d = nc.dram_tensor("l_d", [HQ, 1024], F32, kind=dbg_kind)
    xattn_d = nc.dram_tensor("xattn_d", [D, 1024], F32, kind=dbg_kind)
    xln2_d = nc.dram_tensor("xln2_d", [D, 1024], BF16, kind=dbg_kind)

    tens = dict(xT=xT, wq=wq, wk=wk, wv=wv, wo=wo, gwT=gwT, uwT=uwT, dwT=dwT,
                gaT=gaT, uaT=uaT, daT=daT, gubT=gubT, dbT=dbT,
                cosq=cosq, sinq=sinq, cosk=cosk, sink=sink, tri=tri, rsw=rsw,
                out_fm=out_fm, xln_d=xln_d, kvk_d=kvk_d, kvv_d=kvv_d,
                q_d=q_d, o_d=o_d, l_d=l_d, xattn_d=xattn_d, xln2_d=xln2_d)

    engines = [nc.tensor, nc.vector, nc.scalar, nc.gpsimd, nc.sync]
    pid = nc.partition_id(engines=[e.engine for e in engines])

    sems = dict(
        pe=nc.alloc_semaphore("s_pe"),
        ve=nc.alloc_semaphore("s_ve"),
        se=nc.alloc_semaphore("s_se"),
        bar=[nc.alloc_semaphore(f"s_bar{i}") for i in range(5)],
        dg=[nc.alloc_semaphore(f"s_dg{i}") for i in range(NDMA_G)],
        dsy=[nc.alloc_semaphore(f"s_dsy{i}") for i in range(NDMA_S)],
    )

    for r in nc.Switch(engines=engines, index=[pid] * 5, n=8):
        emit_rank(nc, r, tens, sems)

    return nc


def emit_rank(nc, r, T, sems):
    lo, hi = rank_chunks(r)
    NT = 16 - r
    P2 = NT * CH
    lo0, hi0 = lo * CH, hi * CH

    TE, VE, SE, GP, SY = nc.tensor, nc.vector, nc.scalar, nc.gpsimd, nc.sync

    pe = SemC(sems["pe"])
    ve = SemC(sems["ve"])
    se = SemC(sems["se"])
    dg = [SemC(h) for h in sems["dg"]]
    dsy = [SemC(h) for h in sems["dsy"]]
    bar = [SemC(h) for h in sems["bar"]]
    _bk = [0]

    def barrier():
        _bk[0] += 1
        k = _bk[0]
        TE.nop().then_inc(bar[0].h, 1)
        VE.memset(dummy_v[0:1, 0:1], 0.0).then_inc(bar[1].h, 1)
        SE.memzero(dummy_s[0:1, 0:1]).then_inc(bar[2].h, 1)
        for d in dg:
            GP.wait_ge(d.h, d.n)
        for d in dsy:
            SY.wait_ge(d.h, d.n)
        GP.nop().then_inc(bar[3].h, 1)
        SY.nop().then_inc(bar[4].h, 1)
        for i, e in enumerate([TE, VE, SE, GP, SY]):
            for j in range(5):
                if j != i:
                    e.wait_ge(bar[j].h, k)

    xTap, wqap, wkap, wvap, woap = (T[k].ap() for k in
                                    ("xT", "wq", "wk", "wv", "wo"))

    with (
        nc.sbuf_tensor(f"dummy{r}", [1, 8], F32) as dummy_v,
        nc.sbuf_tensor(f"dummys{r}", [1, 8], F32) as dummy_s,
        nc.sbuf_tensor(f"ones1b{r}", [128, 1], BF16) as ones1b,
        nc.sbuf_tensor(f"ones1f{r}", [1, 128], F32) as ones1f,
        nc.sbuf_tensor(f"onescf{r}", [128, 1], F32) as onescf,
        nc.sbuf_tensor(f"ebias{r}", [128, 1], F32) as ebias,
        nc.sbuf_tensor(f"epst{r}", [1, 1], F32) as epst,
        nc.sbuf_tensor(f"tri{r}", [128, 128], BF16) as tri_sb,
        nc.sbuf_tensor(f"rsw{r}", [128, 128], BF16) as rsw_sb,
    ):
        VE.memset(ones1b[:, :], 1.0)
        VE.memset(ones1f[:, :], 1.0)
        VE.memset(onescf[:, :], 1.0)
        VE.memset(ebias[:, :], EXP_SHIFT)
        VE.memset(epst[:, :], EPS).then_inc(ve.h, 1)
        ve.inc()
        SY.dma_start(out=tri_sb[:, :], in_=T["tri"].ap()).then_inc(dsy[8].h, 16)
        SY.dma_start(out=rsw_sb[:, :], in_=T["rsw"].ap()).then_inc(dsy[8].h, 16)
        dsy[8].inc(32)
        TE.wait_ge(dsy[8].h, dsy[8].n)
        TE.wait_ge(ve.h, ve.n)
        SE.wait_ge(ve.h, ve.n)
        VE.wait_ge(dsy[8].h, dsy[8].n)

        if STOP_AFTER >= 1:
            phase1(nc, r, T, locals())
            barrier()
        if STOP_AFTER >= 2:
            phase2(nc, r, T, locals())
            barrier()
        if STOP_AFTER >= 3:
            phase3(nc, r, T, locals())
            barrier()
        if STOP_AFTER >= 4:
            phase4(nc, r, T, locals())
            barrier()
        if STOP_AFTER >= 5:
            phase56(nc, r, T, locals())
            barrier()
        if STOP_AFTER >= 7:
            phase7(nc, r, T, locals())

        for d in dg:
            GP.wait_ge(d.h, d.n)
        for d in dsy:
            SY.wait_ge(d.h, d.n)


def phase1(nc, r, T, env):
    TE, VE, SE, GP, SY = nc.tensor, nc.vector, nc.scalar, nc.gpsimd, nc.sync
    dg, dsy = env["dg"], env["dsy"]
    pe, ve, se = env["pe"], env["ve"], env["se"]
    ones1b, ones1f, epst = env["ones1b"], env["ones1f"], env["epst"]
    NT = env["NT"]
    s_wk, s_xt = dg[0], dg[1]
    s_rope = [dsy[0], dsy[1]]
    s_xlnst = [dsy[2], dsy[3]]
    s_kst = [dsy[4], dsy[5]]

    with (
        nc.sbuf_tensor(f"p1wk{r}", [128, DC * HKV * HD], BF16) as wk_sb,
        nc.sbuf_tensor(f"p1xt{r}", [128, DC * CH], BF16) as xt,
        nc.sbuf_tensor(f"p1xsq{r}", [128, 4 * CH], BF16) as xsq,
        nc.sbuf_tensor(f"p1xln{r}", [128, 2 * DC * CH], BF16) as xlnb,
        nc.sbuf_tensor(f"p1tmp{r}", [128, CH], BF16) as tmp,
        nc.sbuf_tensor(f"p1tmpf{r}", [128, 2 * CH], F32) as tmpf,
        nc.sbuf_tensor(f"p1kc{r}", [128, CH], BF16) as kc_sb,
        nc.sbuf_tensor(f"p1rows{r}", [1, 6 * CH], F32) as rows,
        nc.sbuf_tensor(f"p1rope{r}", [128, 2 * CH], F32) as ropeb,
        nc.sbuf_tensor(f"p1kst{r}", [128, 2 * HKV * CH], BF16) as kst,
        nc.psum_tensor(f"p1sum{r}", [1, CH], F32) as sum_ps,
        nc.psum_tensor(f"p1sq{r}", [1, CH], F32) as sq_ps,
        nc.psum_tensor(f"p1ab{r}", [128, CH], F32) as ab_ps,
        nc.psum_tensor(f"p1bb{r}", [128, CH], F32) as bb_ps,
        nc.psum_tensor(f"p1k0{r}", [128, CH], F32) as k0_ps,
        nc.psum_tensor(f"p1k1{r}", [128, CH], F32) as k1_ps,
        nc.psum_tensor(f"p1kr{r}", [128, CH], F32) as kr_ps,
    ):
        kps = [k0_ps, k1_ps]
        rsw_sb = env["rsw_sb"]
        GP.dma_start(out=wk_sb[:, :].rearrange("p (c o) -> p c o", c=DC),
                     in_=T["wk"].ap().rearrange("(c p) o -> p c o", p=128)
                     ).then_inc(s_wk.h, 16)
        s_wk.inc(16)
        TE.wait_ge(s_wk.h, s_wk.n)

        xln_done = [None] * NT
        kdma_done = [None] * NT
        rope_done = [None] * NT
        stats_read = [None] * NT
        xlnst_ev = [None] * NT

        sq_cnt = 0

        NTl = min(NT, P1_TILES) if P1_TILES else NT
        for t in range(NTl):
            tc = bass.ts(t, CH)
            if t > 0:
                GP.wait_ge(ve.h, xln_done[t - 1])
            GP.dma_start(out=xt[:, :].rearrange("p (c w) -> p c w", c=DC),
                         in_=xTap_slice(T, tc)).then_inc(s_xt.h, 16)
            s_xt.inc(16)
            ld = s_xt.n

            SE.wait_ge(s_xt.h, ld)
            TE.wait_ge(s_xt.h, ld)
            if t > 0:
                TE.wait_ge(ve.h, stats_read[t - 1])
            for c in range(DC):
                xc = xt[:, bass.ts(c, CH)]
                xq = xsq[:, bass.ts(c % 4, CH)]
                if c >= 4:
                    SE.wait_ge(pe.h, mmsq_base + (c - 4) + 1)
                SE.square(xq, xc).then_inc(se.h, 1)
                sq_cnt = se.inc()
                TE.matmul(sum_ps[:, :], ones1b[:, :], xc,
                          start=(c == 0), stop=(c == DC - 1))
                TE.wait_ge(se.h, sq_cnt)
                if c == 0:
                    mmsq_base = pe.n
                TE.matmul(sq_ps[:, :], ones1b[:, :], xq,
                          start=(c == 0), stop=(c == DC - 1)).then_inc(pe.h, 1)
                pe.inc()
            stats_done = pe.n

            mean = rows[:, 0:CH]
            ex2 = rows[:, CH:2 * CH]
            var = rows[:, 2 * CH:3 * CH]
            sqv = rows[:, 3 * CH:4 * CH]
            a_row = rows[:, 4 * CH:5 * CH]
            b_row = rows[:, 5 * CH:6 * CH]
            VE.wait_ge(pe.h, stats_done)
            VE.tensor_scalar_mul(mean, sum_ps[:, :], 1.0 / D)
            VE.tensor_scalar_mul(ex2, sq_ps[:, :], 1.0 / D)
            VE.engine_nop().then_inc(ve.h, 1)
            stats_read[t] = ve.inc()
            VE.tensor_mul(var, mean, mean)
            VE.tensor_sub(var, ex2, var).then_inc(ve.h, 1)
            vv = ve.inc()
            SE.wait_ge(ve.h, vv)
            SE.activation(sqv, var, AF.Sqrt, bias=epst[:, :]).then_inc(se.h, 1)
            sv = se.inc()
            VE.wait_ge(se.h, sv)
            VE.reciprocal(a_row, sqv)
            VE.tensor_mul(b_row, mean, a_row).then_inc(ve.h, 1)
            rows_done = ve.inc()

            TE.wait_ge(ve.h, rows_done)
            if t > 0:
                TE.wait_ge(ve.h, xln_done[t - 1])
            TE.matmul(ab_ps[:, :], ones1f[:, :], a_row, start=True, stop=True)
            TE.matmul(bb_ps[:, :], ones1f[:, :], b_row,
                      start=True, stop=True).then_inc(pe.h, 1)
            bcast_done = pe.inc()

            if P1_NO_XLN:
                VE.engine_nop().then_inc(ve.h, 1)
                xln_done[t] = ve.inc()
                VE.engine_nop().then_inc(ve.h, 1)
                rope_done[t] = ve.inc()
                kproj_done[t] = pe.n if t > 0 else pe.n
                if t == 0:
                    kproj_done = [pe.n] * NT
                continue
            xl = xlnb[:, bass.ts(t % 2, DC * CH)]
            VE.wait_ge(pe.h, bcast_done)
            if t > 1:
                VE.wait_ge(pe.h, kproj_done[t - 2])
                VE.wait_ge(s_xlnst[t % 2].h, xlnst_ev[t - 2])
            for c in range(DC):
                xc = xt[:, bass.ts(c, CH)]
                VE.tensor_mul(tmp[:, :], xc, ab_ps[:, :])
                VE.tensor_sub(xl[:, bass.ts(c, CH)], tmp[:, :], bb_ps[:, :])
            VE.engine_nop().then_inc(ve.h, 1)
            xln_done[t] = ve.inc()

            SY.wait_ge(ve.h, xln_done[t])
            SY.dma_start(out=T["xln_d"].ap()[:, tc].rearrange(
                             "(c p) w -> p c w", p=128),
                         in_=xl.rearrange("p (c w) -> p c w", c=DC)
                         ).then_inc(s_xlnst[t % 2].h, 16)
            xlnst_ev[t] = s_xlnst[t % 2].inc(16)

            rope_cos = ropeb[:, 0:CH]
            rope_sin = ropeb[:, CH:2 * CH]
            if t > 0:
                SY.wait_ge(ve.h, rope_done[t - 1])
            sr = s_rope[0]
            SY.dma_start(out=rope_cos, in_=T["cosk"].ap()[:, tc]).then_inc(sr.h, 16)
            SY.dma_start(out=rope_sin, in_=T["sink"].ap()[:, tc]).then_inc(sr.h, 16)
            sr.inc(32)
            rope_ld = sr.n

            if t == 0:
                kproj_done = [None] * NT
            if P1_NO_K:
                VE.engine_nop().then_inc(ve.h, 1)
                rope_done[t] = ve.inc()
                kproj_done[t] = pe.n
                continue
            ks = kst[:, bass.ts(t % 2, HKV * CH)]
            if t > 1:
                VE.wait_ge(s_kst[t % 2].h, kdma_done[t - 2])
            if t > 0:
                TE.wait_ge(ve.h, rope_done[t - 1])
            TE.wait_ge(ve.h, xln_done[t])
            kc_ev = {}
            rot_mm = {}
            rope_ev = {}
            for h in range(HKV):
                kp = kps[h % 2]
                if h >= 2:
                    TE.wait_ge(ve.h, kc_ev[h - 2])
                for c in range(DC):
                    ins = TE.matmul(kp[:, :],
                                    wk_sb[:, c * HKV * HD + h * HD:
                                          c * HKV * HD + (h + 1) * HD],
                                    xl[:, bass.ts(c, CH)],
                                    start=(c == 0), stop=(c == DC - 1))
                ins.then_inc(pe.h, 1)
                kmm = pe.inc()
                if P1_NO_ROPE:
                    ko = ks[:, bass.ts(h, CH)]
                    VE.wait_ge(pe.h, kmm)
                    VE.tensor_copy(ko, kp[:, :]).then_inc(ve.h, 1)
                    kc_ev[h] = ve.inc()
                    rope_ev[h] = kc_ev[h]
                    rot_mm[h] = pe.n
                    continue
                VE.wait_ge(pe.h, kmm)
                if h >= 1:
                    VE.wait_ge(pe.h, rot_mm[h - 1])
                elif t > 0:
                    VE.wait_ge(pe.h, last_rot_mm)
                VE.tensor_copy(kc_sb[:, :], kp[:, :]).then_inc(ve.h, 1)
                kc_ev[h] = ve.inc()
                TE.wait_ge(ve.h, kc_ev[h])
                if h >= 1:
                    TE.wait_ge(ve.h, rope_ev[h - 1])
                TE.matmul(kr_ps[:, :], rsw_sb[:, :], kc_sb[:, :],
                          start=True, stop=True).then_inc(pe.h, 1)
                rot_mm[h] = pe.inc()
                ko = ks[:, bass.ts(h, CH)]
                VE.wait_ge(pe.h, rot_mm[h])
                VE.wait_ge(sr.h, rope_ld)
                VE.tensor_mul(tmpf[:, 0:CH], kc_sb[:, :], rope_cos)
                VE.tensor_mul(tmpf[:, CH:2 * CH], kr_ps[:, :], rope_sin)
                VE.tensor_add(ko, tmpf[:, 0:CH], tmpf[:, CH:2 * CH]
                              ).then_inc(ve.h, 1)
                rope_ev[h] = ve.inc()
            last_rot_mm = rot_mm[HKV - 1]
            rope_done[t] = ve.n
            kproj_done[t] = pe.n

            if P1_NO_KST:
                kdma_done[t] = s_kst[t % 2].n
                continue
            SY.wait_ge(ve.h, rope_done[t])
            SY.dma_start(out=T["kvk_d"].ap()[:, tc].rearrange(
                             "(h p) w -> p h w", p=128),
                         in_=ks.rearrange("p (h w) -> p h w", h=HKV)
                         ).then_inc(s_kst[t % 2].h, 16)
            kdma_done[t] = s_kst[t % 2].inc(16)


def xTap_slice(T, tc):
    return T["xT"].ap()[:, tc].rearrange("(c p) w -> p c w", p=128)


def phase2(nc, r, T, env):
    TE, VE, SE, GP, SY = nc.tensor, nc.vector, nc.scalar, nc.gpsimd, nc.sync
    dg, dsy = env["dg"], env["dsy"]
    pe, ve = env["pe"], env["ve"]
    NT = env["NT"]
    KV = HKV * HD
    s_wv = dg[0]
    s_xln = [dsy[0], dsy[1]]
    s_vst = [dsy[2], dsy[3]]

    with (
        nc.sbuf_tensor(f"p2wv{r}", [128, DC * KV], BF16) as wv_sb,
        nc.sbuf_tensor(f"p2xln{r}", [128, 2 * DC * CH], BF16) as xlnb,
        nc.sbuf_tensor(f"p2vst{r}", [128, 2 * KV], BF16) as vst,
        nc.psum_tensor(f"p2v0{r}", [128, KV], F32) as v0,
        nc.psum_tensor(f"p2v1{r}", [128, KV], F32) as v1,
    ):
        vps = [v0, v1]
        GP.dma_start(out=wv_sb[:, :].rearrange("p (c o) -> p c o", c=DC),
                     in_=T["wv"].ap().rearrange("(c p) o -> p c o", p=128)
                     ).then_inc(s_wv.h, 16)
        TE.wait_ge(s_wv.h, s_wv.inc(16))

        xln_ld = [None] * NT
        ev_done = {}
        vst_dma = {}
        for t in range(NT):
            tc = bass.ts(t, CH)
            xl = xlnb[:, bass.ts(t % 2, DC * CH)]
            if t > 1:
                SY.wait_ge(pe.h, mm_done[t - 2])
            SY.dma_start(out=xl.rearrange("p (c w) -> p c w", c=DC),
                         in_=T["xln_d"].ap()[:, tc].rearrange(
                             "(c p) w -> p c w", p=128)
                         ).then_inc(s_xln[t % 2].h, 16)
            xln_ld[t] = s_xln[t % 2].inc(16)
            if t == 0:
                mm_done = [None] * NT
            TE.wait_ge(s_xln[t % 2].h, xln_ld[t])
            for s in range(4):
                vp = vps[s % 2]
                key = (t, s)
                prev = (t, s - 2) if s >= 2 else (t - 1, s + 2)
                if prev in ev_done:
                    TE.wait_ge(ve.h, ev_done[prev])
                for c in range(DC):
                    lhsT = xl[:, c * CH + s * 128: c * CH + (s + 1) * 128]
                    for half in range(2):
                        ins = TE.matmul(vp[:, bass.ts(half, CH)],
                                        lhsT,
                                        wv_sb[:, c * KV + half * CH:
                                              c * KV + (half + 1) * CH],
                                        start=(c == 0), stop=(c == DC - 1))
                ins.then_inc(pe.h, 1)
                mmv = pe.inc()
                vs = vst[:, bass.ts(s % 2, KV)]
                pk = (t, s - 2) if s >= 2 else (t - 1, s + 2)
                if pk in vst_dma:
                    VE.wait_ge(s_vst[s % 2].h, vst_dma[pk])
                VE.wait_ge(pe.h, mmv)
                VE.tensor_copy(vs, vp[:, :]).then_inc(ve.h, 1)
                ev_done[key] = ve.inc()
                SY.wait_ge(ve.h, ev_done[key])
                SY.dma_start(out=T["kvv_d"].ap()[t * CH + s * 128:
                                                 t * CH + (s + 1) * 128, :],
                             in_=vs).then_inc(s_vst[s % 2].h, 16)
                vst_dma[key] = s_vst[s % 2].inc(16)
            mm_done[t] = pe.n


def phase3(nc, r, T, env):
    TE, VE, SE, GP, SY = nc.tensor, nc.vector, nc.scalar, nc.gpsimd, nc.sync
    dg, dsy = env["dg"], env["dsy"]
    pe, ve = env["pe"], env["ve"]
    NT, lo = env["NT"], env["lo"]
    s_xo = dsy[0]
    s_wq = [dg[0], dg[1]]
    s_qst = [dsy[1], dsy[2]]

    with (
        nc.sbuf_tensor(f"p3xln{r}", [128, DC * 1024], BF16) as xo,
        nc.sbuf_tensor(f"p3wq{r}", [128, 2 * DC * HD], BF16) as wqb,
        nc.sbuf_tensor(f"p3rope{r}", [128, 2 * 1024], F32) as ropeq,
        nc.sbuf_tensor(f"p3tmp{r}", [128, 2 * 1024], F32) as tmp,
        nc.sbuf_tensor(f"p3qc{r}", [128, 1024], BF16) as qc_sb,
        nc.sbuf_tensor(f"p3qst{r}", [128, 2 * 1024], BF16) as qst,
        nc.psum_tensor(f"p3q0{r}", [128, 1024], F32) as q0,
        nc.psum_tensor(f"p3q1{r}", [128, 1024], F32) as q1,
        nc.psum_tensor(f"p3qr{r}", [128, 1024], F32) as qr_ps,
    ):
        qps = [q0, q1]
        rsw_sb = env["rsw_sb"]
        for i, t in enumerate((lo, NT - 1)):
            SY.dma_start(
                out=xo[:, :].rearrange("p (c u w) -> p c u w", c=DC, u=2
                                       )[:, :, i, :],
                in_=T["xln_d"].ap()[:, bass.ts(t, CH)].rearrange(
                    "(c p) w -> p c w", p=128)).then_inc(s_xo.h, 16)
            s_xo.inc(16)
        SY.dma_start(out=ropeq[:, 0:1024], in_=T["cosq"].ap()).then_inc(s_xo.h, 16)
        SY.dma_start(out=ropeq[:, 1024:2048], in_=T["sinq"].ap()).then_inc(s_xo.h, 16)
        xo_ld = s_xo.inc(32)
        cosb, sinb = ropeq[:, 0:1024], ropeq[:, 1024:2048]

        TE.wait_ge(s_xo.h, xo_ld)
        VE.wait_ge(s_xo.h, xo_ld)
        rope_ev = {}
        qdma_ev = {}
        qc_ev = {}
        rot_mm = {}
        for oc in range(HQ):
            wqt = wqb[:, bass.ts(oc % 2, DC * HD)]
            if oc >= 2:
                GP.wait_ge(pe.h, qmm_ev[oc - 2])
            GP.dma_start(out=wqt.rearrange("p (c o) -> p c o", c=DC),
                         in_=T["wq"].ap()[:, bass.ts(oc, HD)].rearrange(
                             "(c p) o -> p c o", p=128)
                         ).then_inc(s_wq[oc % 2].h, 16)
            wq_ld = s_wq[oc % 2].inc(16)
            qp = qps[oc % 2]
            if oc == 0:
                qmm_ev = [None] * HQ
            TE.wait_ge(s_wq[oc % 2].h, wq_ld)
            if oc >= 2:
                TE.wait_ge(ve.h, qc_ev[oc - 2])
            for c in range(DC):
                for half in range(2):
                    ins = TE.matmul(qp[:, bass.ts(half, CH)],
                                    wqt[:, bass.ts(c, HD)],
                                    xo[:, c * 1024 + half * CH:
                                       c * 1024 + (half + 1) * CH],
                                    start=(c == 0), stop=(c == DC - 1))
            ins.then_inc(pe.h, 1)
            qmm_ev[oc] = pe.inc()
            VE.wait_ge(pe.h, qmm_ev[oc])
            if oc >= 1:
                VE.wait_ge(pe.h, rot_mm[oc - 1])
            VE.tensor_copy(qc_sb[:, :], qp[:, :]).then_inc(ve.h, 1)
            qc_ev[oc] = ve.inc()
            TE.wait_ge(ve.h, qc_ev[oc])
            if oc >= 1:
                TE.wait_ge(ve.h, rope_ev[oc - 1])
            for half in range(2):
                ins = TE.matmul(qr_ps[:, bass.ts(half, CH)], rsw_sb[:, :],
                                qc_sb[:, bass.ts(half, CH)],
                                start=True, stop=True)
            ins.then_inc(pe.h, 1)
            rot_mm[oc] = pe.inc()
            qo = qst[:, bass.ts(oc % 2, 1024)]
            VE.wait_ge(pe.h, rot_mm[oc])
            if oc >= 2:
                VE.wait_ge(s_qst[oc % 2].h, qdma_ev[oc - 2])
            VE.tensor_mul(tmp[:, 0:1024], qc_sb[:, :], cosb)
            VE.tensor_mul(tmp[:, 1024:2048], qr_ps[:, :], sinb)
            VE.tensor_add(qo, tmp[:, 0:1024], tmp[:, 1024:2048]
                          ).then_inc(ve.h, 1)
            rope_ev[oc] = ve.inc()
            SY.wait_ge(ve.h, rope_ev[oc])
            SY.dma_start(out=T["q_d"].ap()[bass.ts(oc, HD), :],
                         in_=qo).then_inc(s_qst[oc % 2].h, 16)
            qdma_ev[oc] = s_qst[oc % 2].inc(16)


def spans_for(c, r):
    pieces = []
    tris = []
    if c < 4 * r:
        pieces.append((0, 512))
    elif c < 4 * r + 4:
        j0 = 128 * (c - 4 * r)
        pieces.append((j0, 512))
        tris.append(j0)
    ch = c - 4 * (15 - r)
    if ch < 0:
        pieces.append((512, 1024))
    elif ch < 4:
        j0 = 128 * ch
        pieces.append((512 + j0, 1024))
        tris.append(512 + j0)
    merged = []
    for p in pieces:
        if merged and merged[-1][1] == p[0]:
            merged[-1] = (merged[-1][0], p[1])
        else:
            merged.append(list(p) and p)
    return merged, tris


def phase4(nc, r, T, env):
    TE, VE, SE, GP, SY = nc.tensor, nc.vector, nc.scalar, nc.gpsimd, nc.sync
    dg, dsy = env["dg"], env["dsy"]
    pe, ve, se = env["pe"], env["ve"], env["se"]
    NT, P2 = env["NT"], env["P2"]
    ones1b, ones1f, ebias, tri_sb = (env["ones1b"], env["ones1f"],
                                     env["ebias"], env["tri_sb"])
    NKC = P2 // 128
    s_kv = [dsy[0], dsy[1]]
    s_q = [dsy[2], dsy[3]]
    s_ost = [dsy[4], dsy[5]]
    s_ldbg = dsy[6]

    with (
        nc.sbuf_tensor(f"p4k{r}", [128, 2 * S], BF16) as kbuf,
        nc.sbuf_tensor(f"p4v{r}", [128, 2 * S], BF16) as vbuf,
        nc.sbuf_tensor(f"p4q{r}", [128, 2 * 1024], BF16) as qbuf,
        nc.sbuf_tensor(f"p4pt{r}", [128, 2 * 1024], BF16) as ptb,
        nc.sbuf_tensor(f"p4lrow{r}", [1, 3 * 1024], F32) as lrow,
        nc.sbuf_tensor(f"p4linv{r}", [128, 1024], F32) as linvb,
        nc.sbuf_tensor(f"p4ost{r}", [128, 2 * 1024], BF16) as ost,
        nc.psum_tensor(f"p4s0{r}", [128, 1024], F32) as s0,
        nc.psum_tensor(f"p4s1{r}", [128, 1024], F32) as s1,
        nc.psum_tensor(f"p4o{r}", [128, 1024], F32) as o_ps,
        nc.psum_tensor(f"p4l{r}", [1, 1024], F32) as l_ps,
    ):
        sps = [s0, s1]
        kv_ld = {}
        kv_freed = {}
        q_ld = {}
        exp_ev = {}
        mask_ev = {}
        pvl_ev = {}
        smm_ev = {}
        evac_ev = {}
        odma_ev = {}

        def load_kv(h, slot):
            SY.dma_start(out=kbuf[:, slot * S: slot * S + P2],
                         in_=T["kvk_d"].ap()[bass.ts(h, HD), 0:P2]
                         ).then_inc(s_kv[slot].h, 16)
            SY.dma_start(out=vbuf[:, slot * S: slot * S + P2].rearrange(
                             "p (c o) -> p c o", o=HD),
                         in_=T["kvv_d"].ap()[0:P2, bass.ts(h, HD)].rearrange(
                             "(c p) o -> p c o", p=128)
                         ).then_inc(s_kv[slot].h, 16)
            kv_ld[h] = s_kv[slot].inc(32)

        load_kv(0, 0)
        for kvh in range(HKV):
            slot = kvh % 2
            if kvh + 1 < HKV:
                if kvh >= 1:
                    SY.wait_ge(pe.h, kv_freed[kvh - 1])
                load_kv(kvh + 1, (kvh + 1) % 2)
            for qh in range(4):
                g = 4 * kvh + qh
                qslot = g % 2
                if g >= 2:
                    SY.wait_ge(pe.h, pvl_last[g - 2])
                SY.dma_start(out=qbuf[:, bass.ts(qslot, 1024)],
                             in_=T["q_d"].ap()[bass.ts(g, HD), :]
                             ).then_inc(s_q[qslot].h, 16)
                q_ld[g] = s_q[qslot].inc(16)
                if g == 0:
                    pvl_last = {}
                qt = qbuf[:, bass.ts(qslot, 1024)]
                TE.wait_ge(s_kv[slot].h, kv_ld[kvh])
                TE.wait_ge(s_q[qslot].h, q_ld[g])
                if g >= 1:
                    TE.wait_ge(ve.h, evac_ev[g - 1])
                c_last_lo = 4 * r + 3

                def emit_pvl(c):
                    spans, tris = spans_for(c, r)
                    pt = ptb[:, bass.ts(c % 2, 1024)]
                    vt = vbuf[:, slot * S + c * 128: slot * S + (c + 1) * 128]
                    TE.wait_ge(se.h, exp_ev[(g, c)])
                    if (g, c) in mask_ev:
                        TE.wait_ge(ve.h, mask_ev[(g, c)])
                    first = (c == 0)
                    for (a, b) in spans:
                        x = a
                        while x < b:
                            e = min(b, (x // 512 + 1) * 512)
                            last = (c == (c_last_lo if x < 512 else NKC - 1))
                            TE.matmul(o_ps[:, x:e], vt, pt[:, x:e],
                                      start=first, stop=last)
                            ins = TE.matmul(l_ps[:, x:e], ones1b[:, :],
                                            pt[:, x:e], start=first, stop=last)
                            x = e
                    ins.then_inc(pe.h, 1)
                    pvl_ev[(g, c)] = pe.inc()

                for c in range(NKC):
                    spans, tris = spans_for(c, r)
                    sp = sps[c % 2]
                    kt = kbuf[:, slot * S + c * 128: slot * S + (c + 1) * 128]
                    key = (g, c)
                    if c >= 2:
                        TE.wait_ge(se.h, exp_ev[(g, c - 2)])
                    elif g >= 1 and c < 2:
                        TE.wait_ge(se.h, exp_last[g - 1])
                    for (a, b) in spans:
                        x = a
                        while x < b:
                            e = min(b, (x // 512 + 1) * 512)
                            ins = TE.matmul(sp[:, x:e], kt, qt[:, x:e],
                                            start=True, stop=True)
                            x = e
                    ins.then_inc(pe.h, 1)
                    smm_ev[key] = pe.inc()
                    pt = ptb[:, bass.ts(c % 2, 1024)]
                    SE.wait_ge(pe.h, smm_ev[key])
                    if c >= 2:
                        SE.wait_ge(pe.h, pvl_ev[(g, c - 2)])
                    for i, (a, b) in enumerate(spans):
                        ins = SE.activation(pt[:, a:b], sp[:, a:b], AF.Exp,
                                            bias=ebias[:, :])
                        if i == len(spans) - 1:
                            ins.then_inc(se.h, 1)
                    exp_ev[key] = se.inc()
                    if tris:
                        VE.wait_ge(se.h, exp_ev[key])
                        for o in tris:
                            VE.tensor_mul(pt[:, o:o + 128], pt[:, o:o + 128],
                                          tri_sb[:, :])
                        VE.engine_nop().then_inc(ve.h, 1)
                        mask_ev[key] = ve.inc()
                    if c >= 1:
                        emit_pvl(c - 1)
                emit_pvl(NKC - 1)
                exp_last = exp_last if g > 0 else {}
                exp_last[g] = exp_ev[(g, NKC - 1)]
                pvl_last[g] = pvl_ev[(g, NKC - 1)]
                if qh == 3:
                    kv_freed[kvh] = pvl_last[g]

                lr = lrow[:, 0:1024]
                li = lrow[:, 1024:2048]
                VE.wait_ge(pe.h, pvl_last[g])
                VE.wait_ge(s_ldbg.h, s_ldbg.n)
                VE.tensor_copy(lr, l_ps[:, :])
                VE.reciprocal(li, lr).then_inc(ve.h, 1)
                vv = ve.inc()
                TE.wait_ge(ve.h, vv)
                bcp = sps[0]
                TE.matmul(bcp[:, 0:512], ones1f[:, :], li[:, 0:512],
                          start=True, stop=True)
                TE.matmul(bcp[:, 512:1024], ones1f[:, :], li[:, 512:1024],
                          start=True, stop=True).then_inc(pe.h, 1)
                bc = pe.inc()
                VE.wait_ge(pe.h, bc)
                VE.tensor_copy(linvb[:, :], bcp[:, :])
                oo = ost[:, bass.ts(g % 2, 1024)]
                if g >= 2:
                    VE.wait_ge(s_ost[g % 2].h, odma_ev[g - 2])
                VE.tensor_mul(oo, o_ps[:, :], linvb[:, :]).then_inc(ve.h, 1)
                evac_ev[g] = ve.inc()
                SY.wait_ge(ve.h, evac_ev[g])
                SY.dma_start(out=T["o_d"].ap()[bass.ts(g, HD), :],
                             in_=oo).then_inc(s_ost[g % 2].h, 16)
                odma_ev[g] = s_ost[g % 2].inc(16)
                SY.dma_start(out=T["l_d"].ap()[g:g + 1, :], in_=li
                             ).then_inc(s_ldbg.h, 16)
                s_ldbg.inc(16)


def phase56(nc, r, T, env):
    TE, VE, SE, GP, SY = nc.tensor, nc.vector, nc.scalar, nc.gpsimd, nc.sync
    dg, dsy = env["dg"], env["dsy"]
    pe, ve, se = env["pe"], env["ve"], env["se"]
    lo0, hi0 = env["lo0"], env["hi0"]
    ones1b, ones1f, epst = env["ones1b"], env["ones1f"], env["epst"]
    onescf = env["onescf"]
    s_o = dsy[0]
    s_wo = [dg[0], dg[1]]
    s_x = [dsy[1], dsy[2]]
    s_xast = [dsy[3], dsy[4]]
    s_x2 = [dsy[5], dsy[6]]
    s_x2st = [dsy[7], dsy[8]]

    with (
        nc.sbuf_tensor(f"p5o{r}", [128, DC * 1024], BF16) as ob,
        nc.sbuf_tensor(f"p5wo{r}", [128, 2 * DC * HD], BF16) as wob,
        nc.sbuf_tensor(f"p5x{r}", [128, 2 * 1024], F32) as xb,
        nc.sbuf_tensor(f"p5xa{r}", [128, 2 * 1024], F32) as xab,
        nc.sbuf_tensor(f"p5sq{r}", [128, 2 * 1024], BF16) as sqb,
        nc.sbuf_tensor(f"p5rows{r}", [1, 6 * 1024], F32) as rows,
        nc.sbuf_tensor(f"p5ab{r}", [128, 2 * 1024], F32) as abb,
        nc.sbuf_tensor(f"p5xl2{r}", [128, 2 * 1024], BF16) as xl2b,
        nc.psum_tensor(f"p5w0{r}", [128, 1024], F32) as w0,
        nc.psum_tensor(f"p5w1{r}", [128, 1024], F32) as w1,
        nc.psum_tensor(f"p5sum{r}", [1, 1024], F32) as sum_ps,
        nc.psum_tensor(f"p5sq2{r}", [1, 1024], F32) as sq_ps,
    ):
        wps = [w0, w1]
        SY.dma_start(out=ob[:, :].rearrange("p (a w) -> p a w", a=DC),
                     in_=T["o_d"].ap().rearrange("(a p) w -> p a w", p=128)
                     ).then_inc(s_o.h, 16)
        o_ld = s_o.inc(16)
        TE.wait_ge(s_o.h, o_ld)

        xld_ev = {}
        wo_ld = {}
        mm_ev = {}
        ev_ev = {}
        sq_ev = {}
        st_ev = {}
        xa_dma = {}

        def load_x(dch, slot):
            x = xb[:, bass.ts(slot, 1024)]
            SY.dma_start(out=x[:, 0:512],
                         in_=T["xT"].ap()[bass.ts(dch, 128),
                                          lo0:lo0 + 512]).then_inc(s_x[slot].h, 16)
            SY.dma_start(out=x[:, 512:1024],
                         in_=T["xT"].ap()[bass.ts(dch, 128),
                                          hi0:hi0 + 512]).then_inc(s_x[slot].h, 16)
            xld_ev[dch] = s_x[slot].inc(32)

        def load_wo(dch, slot):
            GP.dma_start(out=wob[:, bass.ts(slot, DC * HD)].rearrange(
                             "p (c o) -> p c o", c=DC),
                         in_=T["wo"].ap()[:, bass.ts(dch, HD)].rearrange(
                             "(c p) o -> p c o", p=128)).then_inc(s_wo[slot].h, 16)
            wo_ld[dch] = s_wo[slot].inc(16)

        load_x(0, 0)
        load_wo(0, 0)
        for dch in range(DC):
            slot = dch % 2
            if dch + 1 < DC:
                if dch >= 1:
                    GP.wait_ge(pe.h, mm_ev[dch - 1])
                    SY.wait_ge(ve.h, ev_ev[dch - 1])
                load_x(dch + 1, (dch + 1) % 2)
                load_wo(dch + 1, (dch + 1) % 2)
            wp = wps[slot]
            TE.wait_ge(s_wo[slot].h, wo_ld[dch])
            if dch >= 2:
                TE.wait_ge(ve.h, ev_ev[dch - 2])
            for a in range(DC):
                for half in range(2):
                    ins = TE.matmul(wp[:, bass.ts(half, CH)],
                                    wob[:, slot * DC * HD + a * HD:
                                        slot * DC * HD + (a + 1) * HD],
                                    ob[:, a * 1024 + half * CH:
                                       a * 1024 + (half + 1) * CH],
                                    start=(a == 0), stop=(a == DC - 1))
            ins.then_inc(pe.h, 1)
            mm_ev[dch] = pe.inc()
            xa = xab[:, bass.ts(slot, 1024)]
            VE.wait_ge(pe.h, mm_ev[dch])
            VE.wait_ge(s_x[slot].h, xld_ev[dch])
            if dch >= 2:
                VE.wait_ge(s_xast[slot].h, xa_dma[dch - 2])
                VE.wait_ge(pe.h, st_ev[dch - 2])
            VE.tensor_add(xa, wp[:, :], xb[:, bass.ts(slot, 1024)]
                          ).then_inc(ve.h, 1)
            ev_ev[dch] = ve.inc()
            SY.wait_ge(ve.h, ev_ev[dch])
            SY.dma_start(out=T["xattn_d"].ap()[bass.ts(dch, 128), :],
                         in_=xa).then_inc(s_xast[slot].h, 16)
            xa_dma[dch] = s_xast[slot].inc(16)
            sq = sqb[:, bass.ts(slot, 1024)]
            SE.wait_ge(ve.h, ev_ev[dch])
            if dch >= 2:
                SE.wait_ge(pe.h, st_ev[dch - 2])
            SE.square(sq, xa).then_inc(se.h, 1)
            sq_ev[dch] = se.inc()
            TE.wait_ge(se.h, sq_ev[dch])
            for half in range(2):
                hs = bass.ts(half, CH)
                TE.matmul(sum_ps[:, hs], onescf[:, :], xa[:, hs],
                          start=(dch == 0), stop=(dch == DC - 1))
                ins = TE.matmul(sq_ps[:, hs], ones1b[:, :], sq[:, hs],
                                start=(dch == 0), stop=(dch == DC - 1))
            ins.then_inc(pe.h, 1)
            st_ev[dch] = pe.inc()

        mean = rows[:, 0:1024]
        ex2 = rows[:, 1024:2048]
        var = rows[:, 2048:3072]
        sqv = rows[:, 3072:4096]
        a_row = rows[:, 4096:5120]
        b_row = rows[:, 5120:6144]
        VE.wait_ge(pe.h, st_ev[DC - 1])
        VE.tensor_scalar_mul(mean, sum_ps[:, :], 1.0 / D)
        VE.tensor_scalar_mul(ex2, sq_ps[:, :], 1.0 / D)
        VE.tensor_mul(var, mean, mean)
        VE.tensor_sub(var, ex2, var).then_inc(ve.h, 1)
        vv = ve.inc()
        SE.wait_ge(ve.h, vv)
        SE.activation(sqv, var, AF.Sqrt, bias=epst[:, :]).then_inc(se.h, 1)
        sv = se.inc()
        VE.wait_ge(se.h, sv)
        VE.reciprocal(a_row, sqv)
        VE.tensor_mul(b_row, mean, a_row).then_inc(ve.h, 1)
        rows_done = ve.inc()
        TE.wait_ge(ve.h, rows_done)
        for half in range(2):
            hs = bass.ts(half, CH)
            TE.matmul(w0[:, hs], ones1f[:, :], a_row[:, hs],
                      start=True, stop=True)
            ins = TE.matmul(w1[:, hs], ones1f[:, :], b_row[:, hs],
                            start=True, stop=True)
        ins.then_inc(pe.h, 1)
        bc = pe.inc()
        VE.wait_ge(pe.h, bc)
        VE.tensor_copy(abb[:, 0:1024], w0[:, :])
        VE.tensor_copy(abb[:, 1024:2048], w1[:, :]).then_inc(ve.h, 1)
        ab_ok = ve.inc()

        x2ld = {}
        x2ev = {}
        SY.wait_ge(pe.h, st_ev[DC - 1])
        for dch in range(DC):
            slot = dch % 2
            xa = xab[:, bass.ts(slot, 1024)]
            if dch >= 2:
                SY.wait_ge(ve.h, x2ev[dch - 2])
            SY.dma_start(out=xa, in_=T["xattn_d"].ap()[bass.ts(dch, 128), :]
                         ).then_inc(s_x2[slot].h, 16)
            x2ld[dch] = s_x2[slot].inc(16)
            xl2 = xl2b[:, bass.ts(slot, 1024)]
            VE.wait_ge(s_x2[slot].h, x2ld[dch])
            VE.wait_ge(ve.h, ab_ok)
            if dch >= 2:
                VE.wait_ge(s_x2st[slot].h, x2st[dch - 2])
            VE.tensor_mul(xa, xa, abb[:, 0:1024])
            VE.tensor_sub(xl2, xa, abb[:, 1024:2048]).then_inc(ve.h, 1)
            x2ev[dch] = ve.inc()
            if dch == 0:
                x2st = {}
            SY.wait_ge(ve.h, x2ev[dch])
            SY.dma_start(out=T["xln2_d"].ap()[bass.ts(dch, 128), :],
                         in_=xl2).then_inc(s_x2st[slot].h, 16)
            x2st[dch] = s_x2st[slot].inc(16)


def phase7(nc, r, T, env):
    TE, VE, SE, GP, SY = nc.tensor, nc.vector, nc.scalar, nc.gpsimd, nc.sync
    dg, dsy = env["dg"], env["dsy"]
    pe, ve, se = env["pe"], env["ve"], env["se"]
    s_lora, s_xl = dg[0], dsy[0]
    s_gu = [dg[1], dg[2]]
    s_dn = [dg[3], dg[4]]
    s_xa = [dsy[1], dsy[2]]
    s_out = [dsy[3], dsy[4]]

    with (
        nc.sbuf_tensor(f"p7hid{r}", [128, HC * CH], BF16) as hid,
        nc.sbuf_tensor(f"p7shr{r}", [128, 2 * HC * HD], BF16) as shr,
        nc.sbuf_tensor(f"p7gu{r}", [128, 2 * 2 * DC * HD], BF16) as gub,
        nc.sbuf_tensor(f"p7ab{r}", [128, DC * RK * 2], BF16) as lab,
        nc.sbuf_tensor(f"p7da{r}", [128, HC * RK], BF16) as dab,
        nc.sbuf_tensor(f"p7bgu{r}", [RK, 2 * 8 * 2 * HD], BF16) as bgu,
        nc.sbuf_tensor(f"p7db{r}", [RK, D], BF16) as dbb,
        nc.sbuf_tensor(f"p7z{r}", [RK, 3 * CH], BF16) as zb,
        nc.sbuf_tensor(f"p7sil{r}", [128, 2 * CH], BF16) as sil,
        nc.sbuf_tensor(f"p7xa{r}", [128, 2 * CH], F32) as xab,
        nc.sbuf_tensor(f"p7out{r}", [128, 2 * CH], F32) as outb,
        nc.psum_tensor(f"p7g0{r}", [128, CH], F32) as g0,
        nc.psum_tensor(f"p7g1{r}", [128, CH], F32) as g1,
        nc.psum_tensor(f"p7u0{r}", [128, CH], F32) as u0,
        nc.psum_tensor(f"p7u1{r}", [128, CH], F32) as u1,
        nc.psum_tensor(f"p7zps{r}", [RK, 2 * CH], F32) as z_ps,
        nc.psum_tensor(f"p7zd{r}", [RK, CH], F32) as zd_ps,
        nc.psum_tensor(f"p7d0{r}", [128, CH], F32) as d0,
    ):
        xl2 = shr[:, 0:DC * CH]
        dwb = shr
        gps, ups = [g0, g1], [u0, u1]
        GP.dma_start(out=lab[:, 0:DC * RK].rearrange("p (c k) -> p c k", c=DC),
                     in_=T["gaT"].ap().rearrange("(c p) k -> p c k", p=128)
                     ).then_inc(s_lora.h, 16)
        GP.dma_start(out=lab[:, DC * RK:].rearrange("p (c k) -> p c k", c=DC),
                     in_=T["uaT"].ap().rearrange("(c p) k -> p c k", p=128)
                     ).then_inc(s_lora.h, 16)
        GP.dma_start(out=dab[:, :].rearrange("p (c k) -> p c k", c=HC),
                     in_=T["daT"].ap().rearrange("(c p) k -> p c k", p=128)
                     ).then_inc(s_lora.h, 16)
        GP.dma_start(out=dbb[:, :], in_=T["dbT"].ap()).then_inc(s_lora.h, 16)
        lora_ld = s_lora.inc(64)
        TE.wait_ge(s_lora.h, lora_ld)

        prev_tile_pe = None
        for tile in range(2):
            tc = bass.ts(tile, CH)
            if prev_tile_pe is not None:
                SY.wait_ge(pe.h, prev_tile_pe)
            SY.dma_start(out=xl2.rearrange("p (c w) -> p c w", c=DC),
                         in_=T["xln2_d"].ap()[:, tc].rearrange(
                             "(c p) w -> p c w", p=128)).then_inc(s_xl.h, 16)
            xl_ld = s_xl.inc(16)
            TE.wait_ge(s_xl.h, xl_ld)
            if tile == 1:
                TE.wait_ge(ve.h, z_ev)
                TE.wait_ge(ve.h, prev_out_ve)
                VE.wait_ge(pe.h, prev_tile_pe)
            for c in range(DC):
                xc = xl2[:, bass.ts(c, CH)]
                TE.matmul(z_ps[:, 0:CH], lab[:, c * RK:(c + 1) * RK], xc,
                          start=(c == 0), stop=(c == DC - 1))
                ins = TE.matmul(z_ps[:, CH:2 * CH],
                                lab[:, DC * RK + c * RK:
                                    DC * RK + (c + 1) * RK],
                                xc, start=(c == 0), stop=(c == DC - 1))
            ins.then_inc(pe.h, 1)
            zmm = pe.inc()
            VE.wait_ge(pe.h, zmm)
            VE.tensor_copy(zb[:, 0:2 * CH], z_ps[:, :]).then_inc(ve.h, 1)
            z_ev = ve.inc()

            gu_ld = {}
            gumm = {}
            hid_ev = {}
            bgu_ld = {}
            for h in range(HC):
                slot = h % 2
                if h >= 2:
                    GP.wait_ge(pe.h, gumm[h - 2])
                GP.dma_start(
                    out=gub[:, slot * 2 * DC * HD: slot * 2 * DC * HD + DC * HD
                            ].rearrange("p (c o) -> p c o", c=DC),
                    in_=T["gwT"].ap()[:, bass.ts(h, HD)].rearrange(
                        "(c p) o -> p c o", p=128)).then_inc(s_gu[slot].h, 16)
                GP.dma_start(
                    out=gub[:, slot * 2 * DC * HD + DC * HD:
                            (slot + 1) * 2 * DC * HD
                            ].rearrange("p (c o) -> p c o", c=DC),
                    in_=T["uwT"].ap()[:, bass.ts(h, HD)].rearrange(
                        "(c p) o -> p c o", p=128)).then_inc(s_gu[slot].h, 16)
                gu_ld[h] = s_gu[slot].inc(32)
                GSZ = 8
                j, hj = divmod(h, GSZ)
                if hj == 0:
                    if j >= 2:
                        GP.wait_ge(pe.h, gumm[GSZ * (j - 1) - 1])
                    gcols = min(GSZ, HC - j * GSZ) * 2 * HD
                    GP.dma_start(
                        out=bgu[:, (j % 2) * GSZ * 2 * HD:
                                (j % 2) * GSZ * 2 * HD + gcols],
                        in_=T["gubT"].ap()[:, j * GSZ * 2 * HD:
                                           j * GSZ * 2 * HD + gcols]
                        ).then_inc(s_lora.h, 16)
                    bgu_ld[j] = s_lora.inc(16)
                gp_, up_ = gps[slot], ups[slot]
                TE.wait_ge(s_gu[slot].h, gu_ld[h])
                if hj == 0:
                    TE.wait_ge(s_lora.h, bgu_ld[j])
                TE.wait_ge(ve.h, z_ev)
                if h >= 2:
                    TE.wait_ge(ve.h, hid_ev[h - 2])
                gbase = slot * 2 * DC * HD
                for c in range(DC):
                    xc = xl2[:, bass.ts(c, CH)]
                    TE.matmul(gp_[:, :], gub[:, gbase + c * HD:
                                             gbase + (c + 1) * HD],
                              xc, start=(c == 0), stop=False)
                    TE.matmul(up_[:, :], gub[:, gbase + DC * HD + c * HD:
                                             gbase + DC * HD + (c + 1) * HD],
                              xc, start=(c == 0), stop=False)
                bb0 = (j % 2) * GSZ * 2 * HD + hj * 2 * HD
                TE.matmul(gp_[:, :], bgu[:, bb0:bb0 + HD],
                          zb[:, 0:CH], start=False, stop=True)
                ins = TE.matmul(up_[:, :], bgu[:, bb0 + HD:bb0 + 2 * HD],
                                zb[:, CH:2 * CH], start=False, stop=True)
                ins.then_inc(pe.h, 1)
                gumm[h] = pe.inc()
                sl = sil[:, bass.ts(slot, CH)]
                SE.wait_ge(pe.h, gumm[h])
                if h >= 2:
                    SE.wait_ge(ve.h, hid_ev[h - 2])
                SE.activation(sl, gp_[:, :], AF.Silu).then_inc(se.h, 1)
                sil_ev = se.inc()
                hc_ = hid[:, bass.ts(h, CH)]
                VE.wait_ge(se.h, sil_ev)
                VE.tensor_mul(hc_, sl, up_[:, :]).then_inc(ve.h, 1)
                hid_ev[h] = ve.inc()
                TE.wait_ge(ve.h, hid_ev[h])
                if tile == 1 and h == 0:
                    TE.wait_ge(ve.h, zd_ev)
                ins = TE.matmul(zd_ps[:, :], dab[:, h * RK:(h + 1) * RK], hc_,
                                start=(h == 0), stop=(h == HC - 1))
                if h == HC - 1:
                    ins.then_inc(pe.h, 1)
                    zd_mm = pe.inc()
            VE.wait_ge(pe.h, zd_mm)
            VE.tensor_copy(zb[:, 2 * CH:3 * CH], zd_ps[:, :]).then_inc(ve.h, 1)
            zd_ev = ve.inc()

            dn_ld = {}
            dnmm = {}
            out_ev = {}
            xa_ld = {}
            ost_ev = {}
            GP.wait_ge(pe.h, gumm[HC - 1])
            for dch in range(DC):
                slot = dch % 2
                if dch >= 2:
                    GP.wait_ge(pe.h, dnmm[dch - 2])
                GP.dma_start(out=dwb[:, bass.ts(slot, HC * HD)].rearrange(
                                 "p (c o) -> p c o", c=HC),
                             in_=T["dwT"].ap()[:, bass.ts(dch, HD)].rearrange(
                                 "(c p) o -> p c o", p=128)
                             ).then_inc(s_dn[slot].h, 16)
                dn_ld[dch] = s_dn[slot].inc(16)
                xa = xab[:, bass.ts(slot, CH)]
                if dch >= 2:
                    SY.wait_ge(ve.h, out_ev[dch - 2])
                SY.dma_start(out=xa,
                             in_=T["xattn_d"].ap()[bass.ts(dch, 128), tc]
                             ).then_inc(s_xa[slot].h, 16)
                xa_ld[dch] = s_xa[slot].inc(16)
                d_ = d0 if slot == 0 else g0
                TE.wait_ge(s_dn[slot].h, dn_ld[dch])
                TE.wait_ge(ve.h, zd_ev)
                if dch >= 2:
                    TE.wait_ge(ve.h, out_ev[dch - 2])
                dbase = slot * HC * HD
                for hcc in range(HC):
                    TE.matmul(d_[:, :], dwb[:, dbase + hcc * HD:
                                            dbase + (hcc + 1) * HD],
                              hid[:, bass.ts(hcc, CH)],
                              start=(hcc == 0), stop=False)
                ins = TE.matmul(d_[:, :], dbb[:, bass.ts(dch, HD)],
                                zb[:, 2 * CH:3 * CH], start=False, stop=True)
                ins.then_inc(pe.h, 1)
                dnmm[dch] = pe.inc()
                oo = outb[:, bass.ts(slot, CH)]
                VE.wait_ge(pe.h, dnmm[dch])
                VE.wait_ge(s_xa[slot].h, xa_ld[dch])
                if dch >= 2:
                    VE.wait_ge(s_out[slot].h, ost_ev[dch - 2])
                VE.tensor_add(oo, d_[:, :], xa).then_inc(ve.h, 1)
                out_ev[dch] = ve.inc()
                SY.wait_ge(ve.h, out_ev[dch])
                SY.dma_start(out=T["out_fm"].ap()[bass.ts(dch, 128), tc],
                             in_=oo).then_inc(s_out[slot].h, 16)
                ost_ev[dch] = s_out[slot].inc(16)
            prev_tile_pe = dnmm[DC - 1]
            prev_out_ve = out_ev[DC - 1]


def _rsw_mat():
    m = np.zeros((128, 128), dtype=np.float32)
    for d in range(64):
        m[d + 64, d] = -1.0
        m[d, d + 64] = 1.0
    return m.astype(ml_dtypes.bfloat16)


def _rope_tables():
    inv = 1.0 / (10000.0 ** (np.arange(0, HD, 2, dtype=np.float64) / HD))
    t = np.arange(S, dtype=np.float64)
    fr = np.outer(t, inv)
    emb = np.concatenate([fr, fr], axis=1)
    return (np.cos(emb).astype(np.float32).T.copy(),
            np.sin(emb).astype(np.float32).T.copy())


def kernel(L0_attn_hidden_states, L0_attn_Wq, L0_attn_Wk, L0_attn_Wv,
           L0_attn_Wo, L0_ffn_gate_w, L0_ffn_up_w, L0_ffn_down_w,
           L0_ffn_gate_a, L0_ffn_gate_b, L0_ffn_up_a, L0_ffn_up_b,
           L0_ffn_down_a, L0_ffn_down_b):
    x = np.asarray(L0_attn_hidden_states, dtype=np.float32).reshape(S, D)
    xT = np.ascontiguousarray(x.T)
    f32 = lambda a: np.ascontiguousarray(np.asarray(a, dtype=np.float32))
    wq, wk, wv, wo = f32(L0_attn_Wq), f32(L0_attn_Wk), f32(L0_attn_Wv), f32(L0_attn_Wo)
    gwT = np.ascontiguousarray(f32(L0_ffn_gate_w).T)
    uwT = np.ascontiguousarray(f32(L0_ffn_up_w).T)
    dwT = np.ascontiguousarray(f32(L0_ffn_down_w).T)
    gaT = np.ascontiguousarray(f32(L0_ffn_gate_a).T)
    uaT = np.ascontiguousarray(f32(L0_ffn_up_a).T)
    daT = np.ascontiguousarray(f32(L0_ffn_down_a).T)
    gbT = np.ascontiguousarray(f32(L0_ffn_gate_b).T)
    ubT = np.ascontiguousarray(f32(L0_ffn_up_b).T)
    dbT = np.ascontiguousarray(f32(L0_ffn_down_b).T)
    gub = np.stack([gbT.reshape(RK, HC, HD), ubT.reshape(RK, HC, HD)],
                   axis=2).reshape(RK, 2 * HID)
    gub = np.ascontiguousarray(gub)

    cosT, sinT = _rope_tables()
    scale = 1.0 / np.sqrt(np.float32(HD))
    tri = np.triu(np.ones((128, 128), dtype=np.float32))
    tri = tri.astype(ml_dtypes.bfloat16)
    rsw = _rsw_mat()

    nc = build_program(debug=False)

    shared = dict(xT=xT, wq=wq, wk=wk, wv=wv, wo=wo, gwT=gwT, uwT=uwT,
                  dwT=dwT, gaT=gaT, uaT=uaT, daT=daT, gubT=gub, dbT=dbT,
                  cosk=cosT, sink=sinT, tri=tri, rsw=rsw)
    in_maps = []
    for r in range(NCORES):
        lo, hi = rank_chunks(r)
        cols = np.r_[lo * CH:(lo + 1) * CH, hi * CH:(hi + 1) * CH]
        m = dict(shared)
        m["cosq"] = np.ascontiguousarray(cosT[:, cols] * scale)
        m["sinq"] = np.ascontiguousarray(sinT[:, cols] * scale)
        in_maps.append(m)

    res = run_bass_kernel_spmd(nc, in_maps, core_ids=list(range(NCORES)))

    out = np.empty((S, D), dtype=np.float32)
    for r in range(NCORES):
        lo, hi = rank_chunks(r)
        of = res.results[r]["out_fm"]
        out[lo * CH:(lo + 1) * CH] = of[:, 0:CH].T
        out[hi * CH:(hi + 1) * CH] = of[:, CH:1024].T
    return out.reshape(1, S, D)
